# version 20
# baseline (speedup 1.0000x reference)
import contextlib
import math
import numpy as np

import concourse.bass as bass
import concourse.mybir as mybir
from concourse.bass_utils import run_bass_kernel_spmd

F32 = mybir.dt.float32
BF16 = mybir.dt.bfloat16
I8 = mybir.dt.int8
AF = mybir.ActivationFunctionType
ALU = mybir.AluOpType
AX = mybir.AxisListType

NCORES = 8
NI = 16
SJ = 2048
JC = SJ // NCORES
D = 2048
KC = 16
NMEGA = 8
MJ = JC // NMEGA
NTILE = 4
TOK = 512
INTER = 8192
NIC = 64
EPS = 1e-5


class Rec:
    ENGS = ("t", "v", "s", "g", "y")

    def __init__(self, nc, es):
        self.nc = nc
        self.es = es
        self.q = {e: [] for e in self.ENGS}
        self.sems = {}
        self.cnt = {}
        self.lastw = {}
        for name in ("pe", "dve", "act", "pool"):
            self.sems[name] = es.enter_context(nc.semaphore(name))
            self.cnt[name] = 0

    def dsem(self, name):
        if name not in self.sems:
            self.sems[name] = self.es.enter_context(self.nc.semaphore(name))
            self.cnt[name] = 0
        return name

    def op(self, eng, fn, inc=None, by=1):
        if inc is None:
            self.q[eng].append(fn)
            return None
        self.cnt[inc] += by
        h = self.sems[inc]
        self.q[eng].append(lambda e, f=fn, h=h, b=by: f(e).then_inc(h, b))
        return (inc, self.cnt[inc])

    def dma(self, eng, sem, fn):
        return self.op(eng, fn, inc=self.dsem(sem), by=16)

    def wait(self, eng, tok):
        if tok is None:
            return
        sem, val = tok
        key = (eng, sem)
        if self.lastw.get(key, 0) >= val:
            return
        self.lastw[key] = val
        h = self.sems[sem]
        self.q[eng].append(lambda e, h=h, v=val: e.wait_ge(h, v))

    def waits(self, eng, toks):
        for t in toks:
            self.wait(eng, t)

    def now(self, sem):
        return (sem, self.cnt[sem])

    def vs(self, fn):
        tok = self.op("v", fn, inc="dve")
        self.wait("v", tok)
        return tok


class TokSet:

    def __init__(self):
        self.d = {}

    def add(self, tok):
        if tok is None:
            return
        s, v = tok
        if self.d.get(s, -1) < v:
            self.d[s] = v

    def toks(self):
        return [(s, v) for s, v in self.d.items()]

    def clear(self):
        self.d = {}


DBG_BARRIER = set()


def bar(P):
    toks = [P.now(sm) for sm in ("pe", "dve", "act", "pool")]
    for eng in ("t", "v", "s", "g", "y"):
        P.waits(eng, toks)


class T:
    pass


def _alloc(nc, es, t, layer0, layer1):
    sb = lambda name, shape, dt: es.enter_context(nc.sbuf_tensor(name, shape, dt))
    ps = lambda name, shape: es.enter_context(nc.psum_tensor(name, shape, F32))
    din = lambda name, shape, dt: nc.dram_tensor(name, shape, dt, kind="ExternalInput").ap()
    dout = lambda name, shape, dt: nc.dram_tensor(name, shape, dt, kind="ExternalOutput").ap()

    if layer0:
        t.x = din("x", [NI, JC, D], F32)
        t.Wq = din("Wq", [D, D], F32)
        t.Wkv = din("Wkv", [D, 512], F32)
        t.wg8 = din("wg8", [D, INTER], I8)
        t.wu8 = din("wu8", [D, INTER], I8)
        t.wd8 = din("wd8", [INTER, D], I8)
        t.sg_pc = din("sg_pc", [128, 64], F32)
        t.su_pc = din("su_pc", [128, 64], F32)
        t.sd_r = din("sd_r", [1, D], F32)
        t.bkv_r = din("bkv_r", [1, 512], F32)
    if layer1:
        t.wqkv = din("wqkv", [D, 3 * D], F32)
        t.wdense = din("wdense", [D, D], F32)
        t.WT = din("WT", [D, D], F32)
        t.gwAT = din("gwAT", [D, 136], F32)
        t.ballT = din("ballT", [128, D], F32)
        t.cosq = din("cosq", [128, 128], F32)
        t.sinmq = din("sinmq", [128, 128], F32)
        t.cosk = din("cosk", [128, 128], F32)
        t.sinmk = din("sinmk", [128, 128], F32)
        t.maskc = din("maskc", [128, 128], F32)
    t.rows3 = din("rows3", [1, 4104], F32)
    t.identc = din("identc", [128, 128], F32)
    t.onesc = din("onesc", [1, 128], F32)

    if layer0:
        t.wqS = nc.dram_tensor("wqS", [D, D], BF16, kind="Internal").ap()
        t.wkvS = nc.dram_tensor("wkvS", [D, 512], BF16, kind="Internal").ap()
    if layer1:
        t.wqkvS = nc.dram_tensor("wqkvS", [D, 3 * D], BF16, kind="Internal").ap()
        t.WTS = nc.dram_tensor("WTS", [D, D], BF16, kind="Internal").ap()
        t.wdnS = nc.dram_tensor("wdnS", [D, D], BF16, kind="Internal").ap()
    if layer0 and not layer1:
        t.x1o = dout("x1o", [NI, JC, D], F32)
    if layer1 and not layer0:
        t.x1i = din("x1i", [NI, JC, D], F32)
    if layer1:
        t.out = dout("out", [NI, JC, D], F32)

    t.x1 = sb("x1", [128, NTILE * D], F32)
    t.xnT = sb("xnT", [128, KC * TOK], BF16)
    t.big = sb("big", [128, 32768], BF16)
    t.ctxT = sb("ctxT", [128, 64 * 128], BF16)
    t.wC = sb("wC", [128, 12288], BF16)
    t.scr32 = sb("scr32", [128, 3072], F32)
    t.smallc = sb("smallc", [128, 768], F32)
    t.xascT = sb("xascT", [128, NTILE * 128], BF16)
    t.epsc = sb("epsc", [128, 1], F32)
    t.ones_bf = sb("ones_bf", [1, 128], BF16)
    t.id_bf = sb("id_bf", [128, 128], BF16)
    t.mask_bf = sb("mask_bf", [128, 128], BF16)
    t.rows_bf = sb("rows_bf", [1, 4104], BF16)
    t.cos_q = sb("cos_q", [128, 128], F32)
    t.sinm_q = sb("sinm_q", [128, 128], F32)
    t.cos_k = sb("cos_k", [128, 128], F32)
    t.sinm_k = sb("sinm_k", [128, 128], F32)
    t.wg3 = sb("wg3", [128, 2048], BF16)
    t.wu3 = sb("wu3", [128, 2048], BF16)
    t.sg = sb("sg", [128, 64], F32)
    t.su = sb("su", [128, 64], F32)
    t.bkv_bc = sb("bkv_bc", [128, 512], F32)
    t.sdown_bc = sb("sdown_bc", [128, D], F32)
    t.gwAT_bf = sb("gwAT_bf", [128, KC * 136], BF16)
    t.ballT_bf = sb("ballT_bf", [128, D], BF16)

    t.PBIG0 = ps("PBIG0", [128, 1024])
    t.PBIG1 = ps("PBIG1", [128, 1024])
    t.PS4 = ps("PS4", [128, 512])
    t.PS5 = ps("PS5", [128, 512])
    t.PS6 = ps("PS6", [128, 512])
    t.PS7 = ps("PS7", [128, 512])
    t.BK = [t.PBIG0[:, 0:512], t.PBIG0[:, 512:1024],
            t.PBIG1[:, 0:512], t.PBIG1[:, 512:1024],
            t.PS4[:], t.PS5[:], t.PS6[:], t.PS7[:]]


def bf16v(ap):
    return ap.bitcast(BF16)


def big32(t, off, n):
    return t.big[:].bitcast(F32)[:, off:off + n]


def bigbf(t, off, n):
    return t.big[:, off:off + n]


class St:
    def __init__(self):
        self.bank_free = [None] * 8
        self.x1_w = [None] * NTILE
        self.x1_r = [TokSet() for _ in range(NTILE)]
        self.xnT_ready = None
        self.xnT_r = TokSet()
        self.fuse_ready = [None] * NIC
        self.big_w = TokSet()
        self.big_r = TokSet()
        self.wC_r = TokSet()
        self.scr_r = TokSet()
        self.misc = {}


def emit_setup(P, t, st, layer0, layer1):
    if layer0:
        P.dma("y", "ys", lambda e: e.dma_start(t.sg[:], t.sg_pc))
        P.dma("y", "ys", lambda e: e.dma_start(t.su[:], t.su_pc))
        P.dma("y", "ys", lambda e: e.dma_start(t.scr32[0:1, 0:512], t.bkv_r))
        P.dma("y", "ys", lambda e: e.dma_start(t.scr32[0:1, 512:2560], t.sd_r))
    if layer1:
        P.dma("y", "ys", lambda e: e.dma_start(t.cos_q[:], t.cosq))
        P.dma("y", "ys", lambda e: e.dma_start(t.sinm_q[:], t.sinmq))
        P.dma("y", "ys", lambda e: e.dma_start(t.cos_k[:], t.cosk))
        P.dma("y", "ys", lambda e: e.dma_start(t.sinm_k[:], t.sinmk))
    ytok = P.dma("y", "ys", lambda e: e.dma_start(t.scr32[0:1, 2560:2688], t.onesc))
    P.dma("g", "gs", lambda e: e.dma_start(t.id_bf[:], t.identc))
    P.dma("g", "gs", lambda e: e.dma_start(t.ones_bf[:], t.onesc))
    gtok = P.dma("g", "gs", lambda e: e.dma_start(t.rows_bf[:], t.rows3))
    if layer1:
        P.dma("g", "gs", lambda e: e.dma_start(t.mask_bf[:], t.maskc))
        P.dma("g", "gs", lambda e: e.dma_start(
            t.gwAT_bf[:].rearrange("p (k n) -> p k n", n=136),
            t.gwAT.rearrange("(k p) n -> p k n", p=128)))
        gtok = P.dma("g", "gs", lambda e: e.dma_start(t.ballT_bf[:], t.ballT))
    st.misc["setup_g"] = gtok
    st.misc["setup_y"] = ytok
    st.misc["eps_ready"] = P.op("v", lambda e: e.memset(t.epsc[:], EPS), inc="dve")

    stage = [t.ctxT[:, i * 2048:(i + 1) * 2048] for i in range(4)]
    out_tok = [None] * 4
    ci = [0]

    def convert(src, dst, ncols):
        last = None
        for kc in range(KC):
            for b0 in range(0, ncols, 2048):
                w = min(2048, ncols - b0)
                pb = ci[0] % 4
                ci[0] += 1
                P.wait("g", out_tok[pb])
                ti = P.dma("g", "cv%d" % pb, lambda e, kc=kc, b0=b0, w=w, pb=pb,
                           src=src: e.dma_start(
                               stage[pb][:, 0:w],
                               src.rearrange("(k p) n -> p k n", p=128)
                               [:, kc, b0:b0 + w]))
                P.wait("y", ti)
                last = P.dma("y", "cvo%d" % pb, lambda e, kc=kc, b0=b0, w=w, pb=pb,
                             dst=dst: e.dma_start(
                                 dst.rearrange("(k p) n -> p k n", p=128)
                                 [:, kc, b0:b0 + w],
                                 stage[pb][:, 0:w]))
                out_tok[pb] = last
        return last

    st.misc["convert_fn"] = convert
    st.misc["conv_out_tok"] = out_tok
    if layer0:
        st.misc[("conv", "wkv")] = convert(t.Wkv, t.wkvS, 512)
        st.misc[("conv", "wq")] = convert(t.Wq, t.wqS, D)
    if layer1 and not layer0:
        st.misc[("conv", "wqkv")] = convert(t.wqkv, t.wqkvS, 3 * D)
        st.misc[("conv", "wdn")] = convert(t.wdense, t.wdnS, D)
        st.misc[("conv", "WT")] = convert(t.WT, t.WTS, D)
    st.misc["stage_free"] = out_tok

    if layer0:
        ones32 = t.scr32[0:1, 2560:2688]
        P.wait("t", ytok)
        tok = P.op("t", lambda e: e.matmul(t.BK[4], ones32, t.scr32[0:1, 0:512],
                                           start=True, stop=True), inc="pe")
        P.wait("v", tok)
        tok2 = P.op("v", lambda e: e.tensor_copy(t.bkv_bc[:], t.BK[4]), inc="dve")
        st.bank_free[4] = tok2
        for oc in range(4):
            P.wait("t", st.bank_free[5])
            tok = P.op("t", lambda e, oc=oc: e.matmul(
                t.BK[5], ones32, t.scr32[0:1, 512 + oc * 512:512 + (oc + 1) * 512],
                start=True, stop=True), inc="pe")
            P.wait("v", tok)
            tok2 = P.op("v", lambda e, oc=oc: e.tensor_copy(
                t.sdown_bc[:, oc * 512:(oc + 1) * 512], t.BK[5]), inc="dve")
            st.bank_free[5] = tok2
        st.scr_r.add(P.now("pe"))


def emit_x_load(P, t, st, m, src):
    for tt in range(NTILE):
        j0 = m * MJ + tt * 8
        P.waits("y", st.x1_r[tt].toks())
        P.wait("y", st.x1_w[tt])
        tok = P.dma("y", "x%d" % tt, lambda e, j0=j0, tt=tt: e.dma_start(
            t.x1[:, tt * D:(tt + 1) * D], src.transpose([1, 0, 2])[j0:j0 + 8]))
        st.x1_w[tt] = tok
        st.x1_r[tt].clear()


def emit_ln(P, t, st, m, carve):
    if carve == "big":
        scr = big32(t, 0, 2048)
        xn_tm = [bigbf(t, 4096 + p * 2048, 2048) for p in (0, 1)]
        region = st.big_r
    else:
        wc32 = t.wC[:].bitcast(F32)
        scr = wc32[:, 0:2048]
        xn_tm = [t.wC[:, 4096 + p * 2048:4096 + (p + 1) * 2048] for p in (0, 1)]
        region = st.wC_r
    war = region.toks()
    region.clear()

    R = {}
    for tt in range(NTILE):
        S = t.smallc[:, 640 + tt * 8:648 + tt * 8]
        R[tt] = tuple(S[:, i:i + 1] for i in range(8))
    a1t = {}
    drt = {}
    d1t = {}
    a2t = {}
    stats = {}
    for tt in range(NTILE):
        s2 = R[tt][0]
        stats_free = st.misc.get(("ln_stats_free", tt))
        P.wait("s", st.x1_w[tt])
        P.waits("s", war)
        P.wait("s", stats_free)
        a1t[tt] = P.op("s", lambda e, tt=tt, scr=scr, s2=s2: e.activation(
            scr, t.x1[:, tt * D:(tt + 1) * D], AF.Square, accum_out=s2), inc="act")
        P.wait("v", st.x1_w[tt])
        P.wait("v", stats_free)
        drt[tt] = P.op("v", lambda e, tt=tt, s1=R[tt][1]: e.reduce_sum(
            s1, t.x1[:, tt * D:(tt + 1) * D], axis=AX.X), inc="dve")
    for tt in range(NTILE):
        s2, s1, mean, ex2, nvar = R[tt][0:5]
        P.wait("v", drt[tt])
        P.vs(lambda e, mean=mean, s1=s1: e.tensor_scalar_mul(mean, s1, 1.0 / D))
        P.wait("v", a1t[tt])
        P.vs(lambda e, ex2=ex2, s2=s2: e.tensor_scalar_mul(ex2, s2, 1.0 / D))
        d1t[tt] = P.op("v", lambda e, nvar=nvar, mean=mean, ex2=ex2:
                       e.scalar_tensor_tensor(
                           nvar, mean, mean, ex2, ALU.mult, ALU.subtract), inc="dve")
    for tt in range(NTILE):
        nvar, stdc = R[tt][4], R[tt][5]
        P.wait("s", d1t[tt])
        P.wait("s", st.misc.get("eps_ready"))
        a2t[tt] = P.op("s", lambda e, stdc=stdc, nvar=nvar: e.activation(
            stdc, nvar, AF.Sqrt, scale=-1.0, bias=t.epsc[:]), inc="act")
    for tt in range(NTILE):
        mean, stdc, rstd, nmr = R[tt][2], R[tt][5], R[tt][6], R[tt][7]
        P.wait("v", a2t[tt])
        P.vs(lambda e, rstd=rstd, stdc=stdc: e.reciprocal(rstd, stdc))
        d2 = P.op("v", lambda e, nmr=nmr, mean=mean, rstd=rstd: e.tensor_scalar(
            nmr, mean, rstd, -1.0, ALU.mult, ALU.mult), inc="dve")
        P.wait("v", d2)
        stats[tt] = (rstd, nmr, d2)

    xn_tok = [None, None]
    evac_last = None
    for tt in range(NTILE):
        par = tt % 2
        rstd, nmr, d2 = stats[tt]
        P.wait("s", d2)
        P.wait("s", xn_tok[par])
        a3 = P.op("s", lambda e, tt=tt, par=par, rstd=rstd, nmr=nmr: e.activation(
            xn_tm[par], t.x1[:, tt * D:(tt + 1) * D], AF.Identity, scale=rstd, bias=nmr),
            inc="act")
        st.misc[("ln_stats_free", tt)] = a3
        st.x1_r[tt].add(a3)
        st.x1_r[tt].add(d2)
        for half in range(2):
            bank = 6 + half
            P.wait("t", a3)
            P.wait("t", st.bank_free[bank])
            ptile = bf16v(t.BK[bank])
            tok = None
            for j in range(8):
                kc = half * 8 + j
                tok = P.op("t", lambda e, ptile=ptile, j=j, kc=kc, par=par: e.matmul(
                    ptile[:, j * 128:(j + 1) * 128],
                    xn_tm[par][:, kc * 128:(kc + 1) * 128], t.id_bf[:],
                    is_transpose=True, start=(j == 0), stop=(j == 7)),
                    inc="pe" if j == 7 else None)
            if half == 1:
                xn_tok[par] = tok
            region.add(tok)
            P.wait("v", tok)
            P.waits("v", st.xnT_r.toks())
            ev = P.op("v", lambda e, ptile=ptile, half=half, tt=tt: e.tensor_copy(
                t.xnT[:].rearrange("p (k n) -> p k n", n=TOK)
                [:, half * 8:(half + 1) * 8, tt * 128:(tt + 1) * 128],
                ptile[:].rearrange("p (k n) -> p k n", n=128)), inc="dve")
            st.bank_free[bank] = ev
            evac_last = ev
    region.add(P.now("act"))
    st.xnT_ready = evac_last


def emit_attn0(P, t, st, m):
    wkv = t.wC[:, 0:8192]
    wqb = [bigbf(t, 16384, 8192), bigbf(t, 24576, 8192)]
    war = st.big_r.toks()
    st.big_r.clear()
    P.waits("g", st.wC_r.toks())
    P.wait("g", st.misc.get(("conv", "wkv")))
    tok_wkv = P.dma("g", "wkv", lambda e: e.dma_start(
        wkv.rearrange("p (k n) -> p k n", n=512),
        t.wkvS.rearrange("(k p) n -> p k n", p=128)))

    kvsb = [t.scr32[:, tt * 512:(tt + 1) * 512] for tt in range(NTILE)]
    prod = [t.scr32[:, 2048:2560], t.scr32[:, 2560:3072]]
    s_all = [t.smallc[:, 512 + tt * 32:512 + (tt + 1) * 32] for tt in range(NTILE)]
    scr_war = st.scr_r.toks()
    st.scr_r.clear()

    tok_kv = [None] * NTILE
    for tt in range(NTILE):
        P.wait("t", tok_wkv)
        P.wait("t", st.xnT_ready)
        P.wait("t", st.bank_free[4])
        tokkv = None
        for kc in range(KC):
            tokkv = P.op("t", lambda e, kc=kc, tt=tt: e.matmul(
                t.BK[4], t.xnT[:, kc * TOK + tt * 128:kc * TOK + tt * 128 + 128],
                wkv.rearrange("p (k n) -> p k n", n=512)[:, kc, :],
                start=(kc == 0), stop=(kc == 15)), inc="pe" if kc == 15 else None)
        P.wait("v", tokkv)
        P.waits("v", scr_war)
        P.wait("v", st.misc.get("kv_free"))
        tok_kv[tt] = P.op("v", lambda e, tt=tt: e.tensor_add(
            kvsb[tt], t.BK[4], t.bkv_bc[:]), inc="dve")
        st.bank_free[4] = tok_kv[tt]
    st.misc["wkv_pe"] = tokkv
    st.wC_r.add(tokkv)

    for g4 in range(4):
        pb = g4 % 2
        if g4 == 0:
            P.waits("g", war)
        P.wait("g", st.misc.get(("wq_pe", pb)))
        P.wait("g", st.misc.get(("conv", "wq")))
        tok_wq = P.dma("g", "wq%d" % pb, lambda e, g4=g4, pb=pb: e.dma_start(
            wqb[pb].rearrange("p (k n) -> p k n", n=512),
            t.wqS.rearrange("(k p) n -> p k n", p=128)[:, :, g4 * 512:(g4 + 1) * 512]))
        for tt in range(NTILE):
            par = (g4 * NTILE + tt) % 2
            qb = par % 2
            P.wait("t", tok_wq)
            P.wait("t", st.bank_free[qb])
            for kc in range(KC):
                P.op("t", lambda e, kc=kc, tt=tt, pb=pb, qb=qb: e.matmul(
                    t.BK[qb], t.xnT[:, kc * TOK + tt * 128:kc * TOK + tt * 128 + 128],
                    wqb[pb].rearrange("p (k n) -> p k n", n=512)[:, kc, :],
                    start=(kc == 0), stop=False))
            tokq = P.op("t", lambda e, g4=g4, qb=qb: e.matmul(
                t.BK[qb], t.ones_bf[:], t.rows_bf[:, g4 * 512:(g4 + 1) * 512],
                start=False, stop=True), inc="pe")
            st.misc[("wq_pe", pb)] = tokq
            P.wait("v", tokq)
            P.wait("v", tok_kv[tt])
            P.wait("v", st.misc.get(("prod_free", par)))
            P.op("v", lambda e, qb=qb, par=par, g4=g4, tt=tt: e.tensor_tensor(
                prod[par].rearrange("p (h w) -> p h w", w=64),
                t.BK[qb].rearrange("p (h w) -> p h w", w=64),
                kvsb[tt][:, g4 * 128:g4 * 128 + 64].unsqueeze(1).broadcast_to((128, 8, 64)),
                ALU.mult))
            tokp = P.op("v", lambda e, par=par, g4=g4, tt=tt: e.reduce_sum(
                s_all[tt][:, g4 * 8:(g4 + 1) * 8],
                prod[par].rearrange("p (h w) -> p h w", w=64), axis=AX.X), inc="dve")
            st.bank_free[qb] = tokp
            st.misc[("prod_free", par)] = tokp
            st.misc[("s_red", tt)] = tokp

    tokx = None
    for tt in range(NTILE):
        par = tt % 2
        C = t.smallc[:, par * 256:(par + 1) * 256]
        z32, e32, at32, r32 = (C[:, 8 + 32 * i:8 + 32 * (i + 1)] for i in range(4))
        P.wait("v", st.misc.get(("attn_free", par)))
        P.wait("v", st.misc.get(("s_red", tt)))
        P.vs(lambda e, tt=tt, z32=z32: e.tensor_scalar_mul(z32, s_all[tt], 0.125))
        dz = P.vs(lambda e, z32=z32: e.tensor_sub(z32, z32, z32))
        P.wait("s", dz)
        ae = P.op("s", lambda e, e32=e32, z32=z32: e.activation(e32, z32, AF.Exp), inc="act")
        P.wait("v", ae)
        P.vs(lambda e, r32=r32, e32=e32: e.reciprocal(r32, e32))
        da = P.vs(lambda e, at32=at32, e32=e32, r32=r32: e.tensor_mul(at32, e32, r32))
        P.waits("v", st.x1_r[tt].toks())
        P.wait("v", st.x1_w[tt])
        for g4 in range(4):
            for h in range(8):
                c0 = g4 * 512 + h * 64
                tokx = P.op("v", lambda e, tt=tt, g4=g4, c0=c0, at32=at32, h=h:
                            e.scalar_tensor_tensor(
                                t.x1[:, tt * D + c0:tt * D + c0 + 64],
                                kvsb[tt][:, g4 * 128 + 64:(g4 + 1) * 128],
                                at32[:, g4 * 8 + h:g4 * 8 + h + 1],
                                t.x1[:, tt * D + c0:tt * D + c0 + 64],
                                ALU.mult, ALU.add),
                            inc="dve" if (g4 == 3 and h == 7) else None)
        st.x1_w[tt] = tokx
        st.x1_r[tt].clear()
        st.misc[("attn_free", par)] = tokx
    st.misc["kv_free"] = tokx
    st.big_r.add(P.now("pe"))
    st.scr_r.add(P.now("dve"))
    st.scr_r.add(tokx)


def emit_ffn0(P, t, st, m):
    war = st.big_r.toks()
    st.big_r.clear()
    scr_war = st.scr_r.toks()
    st.scr_r.clear()
    wgb = [t.wC[:, 0:2048], t.wC[:, 2048:4096], t.wg3[:]]
    wub = [t.wC[:, 4096:6144], t.wC[:, 6144:8192], t.wu3[:]]
    wdb = [t.wC[:, 8192:10240], t.wC[:, 10240:12288]]
    silu = [t.scr32[:, 0:512], t.scr32[:, 512:1024]]
    tmpd = [t.scr32[:, 1024:1536], t.scr32[:, 1536:2048]]
    wc_war = st.wC_r.toks()
    st.wC_r.clear()

    gu_tok = {}

    def issue_gu(ic):
        sl = ic % 3
        P.waits("g", wc_war)
        P.wait("g", st.misc.get(("wg_pe", sl)))
        tg = P.dma("g", "wg%d" % sl, lambda e, ic=ic, sl=sl: e.dma_start(
            wgb[sl].rearrange("p (k n) -> p k n", n=128),
            t.wg8.rearrange("(k p) n -> p k n", p=128)[:, :, ic * 128:(ic + 1) * 128]))
        P.wait("g", st.misc.get(("wu_pe", sl)))
        tu = P.dma("g", "wu%d" % sl, lambda e, ic=ic, sl=sl: e.dma_start(
            wub[sl].rearrange("p (k n) -> p k n", n=128),
            t.wu8.rearrange("(k p) n -> p k n", p=128)[:, :, ic * 128:(ic + 1) * 128]))
        gu_tok[ic] = (tg, tu)

    issue_gu(0)
    issue_gu(1)
    for ic in range(NIC):
        sl = ic % 3
        pb = ic % 2
        bg = pb
        bu = 2 + pb
        if ic + 2 < NIC:
            issue_gu(ic + 2)
        tok_wg, tok_wu = gu_tok.pop(ic)
        P.wait("t", tok_wg)
        P.wait("t", st.xnT_ready)
        P.wait("t", st.bank_free[bg])
        tokg = None
        for kc in range(KC):
            tokg = P.op("t", lambda e, kc=kc, sl=sl, bg=bg: e.matmul(
                t.BK[bg], wgb[sl][:, kc * 128:(kc + 1) * 128],
                t.xnT[:, kc * TOK:(kc + 1) * TOK],
                start=(kc == 0), stop=(kc == 15)), inc="pe" if kc == 15 else None)
        st.misc[("wg_pe", sl)] = tokg
        P.wait("t", tok_wu)
        P.wait("t", st.bank_free[bu])
        toku = None
        for kc in range(KC):
            toku = P.op("t", lambda e, kc=kc, sl=sl, bu=bu: e.matmul(
                t.BK[bu], wub[sl][:, kc * 128:(kc + 1) * 128],
                t.xnT[:, kc * TOK:(kc + 1) * TOK],
                start=(kc == 0), stop=(kc == 15)), inc="pe" if kc == 15 else None)
        st.misc[("wu_pe", sl)] = toku
        P.wait("s", tokg)
        P.waits("s", scr_war)
        P.wait("s", st.misc.get("setup_y"))
        P.wait("s", st.misc.get(("silu_free", pb)))
        tsil = P.op("s", lambda e, ic=ic, pb=pb, bg=bg: e.activation(
            silu[pb], t.BK[bg], AF.Silu, scale=t.sg[:, ic:ic + 1]), inc="act")
        st.bank_free[bg] = tsil
        P.wait("v", toku)
        P.wait("v", tsil)
        if ic < 2:
            P.waits("v", war)
            P.waits("v", scr_war)
            P.wait("v", st.misc.get("setup_y"))
        tfu = P.op("v", lambda e, ic=ic, pb=pb, bu=bu: e.scalar_tensor_tensor(
            bigbf(t, ic * 512, 512), t.BK[bu], t.su[:, ic:ic + 1], silu[pb],
            ALU.mult, ALU.mult), inc="dve")
        st.bank_free[bu] = tfu
        st.misc[("silu_free", pb)] = tfu
        st.fuse_ready[ic] = tfu

    wdslot = [t.wC[:, i * 2048:(i + 1) * 2048] for i in range(6)]
    slot_tok = {0: st.misc.get(("wg_pe", 0)), 1: st.misc.get(("wg_pe", 1)),
                2: st.misc.get(("wu_pe", 0)), 3: st.misc.get(("wu_pe", 1)),
                4: st.misc.get(("wd_pe", 0)), 5: st.misc.get(("wd_pe", 1))}
    wd_tok = {}

    def issue_wd(icg):
        ic = icg % NIC
        sl = (icg + 4) % 6
        P.wait("g", slot_tok.get(sl))
        P.waits("g", wc_war)
        wd_tok[icg] = (sl, P.dma("g", "wd%d" % sl, lambda e, ic=ic, sl=sl:
                               e.dma_start(wdslot[sl], t.wd8[ic * 128:(ic + 1) * 128, :])))

    issue_wd(0)
    issue_wd(1)
    for half in range(2):
        tsubs = (2 * half, 2 * half + 1)
        bank_tok = [None] * 8
        for ic in range(NIC):
            icg = half * NIC + ic
            if icg + 2 < 2 * NIC:
                issue_wd(icg + 2)
            sl, tok_wd = wd_tok.pop(icg)
            P.wait("t", tok_wd)
            P.wait("t", st.fuse_ready[ic])
            tk = None
            for ti, tsub in enumerate(tsubs):
                for oc in range(4):
                    bk = ti * 4 + oc
                    if ic == 0:
                        P.wait("t", st.bank_free[bk])
                    tk = P.op("t", lambda e, ic=ic, tsub=tsub, oc=oc, sl=sl, bk=bk:
                              e.matmul(
                                  t.BK[bk], bigbf(t, ic * 512 + tsub * 128, 128),
                                  wdslot[sl][:, oc * 512:(oc + 1) * 512],
                                  start=(ic == 0), stop=(ic == NIC - 1)),
                              inc="pe" if (ic == NIC - 1 or (ti == 1 and oc == 3))
                              else None)
                    if ic == NIC - 1:
                        bank_tok[bk] = tk
            slot_tok[sl] = tk
        for ti, tsub in enumerate(tsubs):
            for oc in range(4):
                bk = ti * 4 + oc
                par = oc % 2
                P.wait("v", bank_tok[bk])
                P.wait("v", st.misc.get(("tmpd_free", par)))
                P.op("v", lambda e, bk=bk, oc=oc, par=par: e.tensor_mul(
                    tmpd[par], t.BK[bk], t.sdown_bc[:, oc * 512:(oc + 1) * 512]))
                P.waits("v", st.x1_r[tsub].toks())
                P.wait("v", st.x1_w[tsub])
                tokx = P.op("v", lambda e, tsub=tsub, oc=oc, par=par: e.tensor_add(
                    t.x1[:, tsub * D + oc * 512:tsub * D + (oc + 1) * 512],
                    t.x1[:, tsub * D + oc * 512:tsub * D + (oc + 1) * 512],
                    tmpd[par]), inc="dve")
                st.bank_free[bk] = tokx
                st.misc[("tmpd_free", par)] = tokx
                st.x1_w[tsub] = tokx
                st.x1_r[tsub].clear()
    st.misc[("wd_pe", 0)] = slot_tok[4]
    st.misc[("wd_pe", 1)] = slot_tok[5]
    st.misc[("wg_pe", 0)] = slot_tok[0]
    st.misc[("wg_pe", 1)] = slot_tok[1]
    st.misc[("wu_pe", 0)] = slot_tok[2]
    st.misc[("wu_pe", 1)] = slot_tok[3]
    st.big_r.add(P.now("pe"))
    st.wC_r.add(P.now("pe"))
    st.scr_r.add(P.now("dve"))
    st.scr_r.add(P.now("act"))


def emit_x1_store(P, t, st, m):
    for tt in range(NTILE):
        j0 = m * MJ + tt * 8
        P.wait("y", st.x1_w[tt])
        tok = P.dma("y", "xs%d" % tt, lambda e, j0=j0, tt=tt: e.dma_start(
            t.x1o.transpose([1, 0, 2])[j0:j0 + 8], t.x1[:, tt * D:(tt + 1) * D]))
        st.x1_r[tt].add(tok)


def emit_l1attn(P, t, st, m):
    war = st.big_r.toks()
    st.big_r.clear()
    scr_war = st.scr_r.toks()
    st.scr_r.clear()
    wqkvb = [bigbf(t, 0, 12288), bigbf(t, 12288, 12288)]
    FB = 24576
    qr = [bigbf(t, FB + s * 256, 256) for s in range(4)]
    kr = [bigbf(t, FB + 1024 + s * 256, 256) for s in range(4)]
    v2 = [bigbf(t, FB + 2048 + s * 256, 256) for s in range(8)]
    qkT = [bigbf(t, FB + 4096 + s * 512, 512) for s in range(4)]
    pbf = [bigbf(t, FB + 6144 + s * 256, 256) for s in range(4)]
    pT = [bigbf(t, FB + 7168 + s * 256, 256) for s in range(4)]
    q2f = [t.scr32[:, sl * 1280:sl * 1280 + 256] for sl in range(2)]
    k2f = [t.scr32[:, sl * 1280 + 256:sl * 1280 + 512] for sl in range(2)]
    tmpq = [t.scr32[:, sl * 1280 + 512:sl * 1280 + 768] for sl in range(2)]
    qcf = [t.scr32[:, sl * 1280 + 768:sl * 1280 + 1024] for sl in range(2)]
    e2f = [t.scr32[:, sl * 1280 + 1024:sl * 1280 + 1280] for sl in range(2)]

    ctxT3 = t.ctxT[:].rearrange("p (s n) -> p s n", n=128)
    for tok in st.misc.get("stage_free", ()):
        P.wait("s", tok)
    P.waits("s", war)
    P.wait("v", st.misc.get("setup_y"))
    P.wait("g", st.misc.get("setup_y"))
    for s in range(4):
        st.misc[("mid_free", s)] = st.bank_free[4 + s]

    pre_wq = {}
    for hp in range(2):
        P.waits("y", war)
        P.wait("y", st.misc.get(("wqkv_pe", hp)))
        P.wait("y", st.misc.get(("conv", "wqkv")))
        pre_wq[hp] = P.dma("y", "wqkvy%d" % hp, lambda e, hp=hp: e.dma_start(
            wqkvb[hp].rearrange("p (k n) -> p k n", n=768),
            t.wqkvS.rearrange("(k p) n -> p k n", p=128)
            [:, :, hp * 768:(hp + 1) * 768]))

    NIT = 8 * NTILE
    state = {}
    tevs = {}
    tok_wq = [None]

    def stageA(i):
        hp, tt = i // NTILE, i % NTILE
        pb = hp % 2
        s4 = i % 4
        s8 = i % 8
        if tt == 0:
            if hp < 2:
                tok_wq[0] = pre_wq[hp]
            else:
                P.wait("g", st.misc.get(("wqkv_pe", pb)))
                P.wait("g", st.misc.get(("conv", "wqkv")))
                tok_wq[0] = P.dma("g", "wqkv%d" % pb, lambda e, hp=hp, pb=pb:
                                  e.dma_start(
                    wqkvb[pb].rearrange("p (k n) -> p k n", n=768),
                    t.wqkvS.rearrange("(k p) n -> p k n", p=128)
                    [:, :, hp * 768:(hp + 1) * 768]))
        PQ = t.PBIG0 if i % 2 == 0 else t.PBIG1
        bks = (0, 1) if i % 2 == 0 else (2, 3)
        P.wait("t", tok_wq[0])
        P.wait("t", st.xnT_ready)
        P.wait("t", st.bank_free[bks[0]])
        P.wait("t", st.bank_free[bks[1]])
        P.wait("t", st.misc.get(("qtb_free", i % 2)))
        tokqkv = None
        for kc in range(KC):
            for h01 in range(2):
                tokqkv = P.op("t", lambda e, kc=kc, tt=tt, h01=h01, pb=pb, PQ=PQ:
                              e.matmul(
                                  PQ[:, h01 * 512:h01 * 512 + 384],
                                  t.xnT[:, kc * TOK + tt * 128:kc * TOK + tt * 128 + 128],
                                  wqkvb[pb].rearrange("p (k n) -> p k n", n=768)
                                  [:, kc, h01 * 384:(h01 + 1) * 384],
                                  start=(kc == 0), stop=(kc == 15)),
                              inc="pe" if (kc == 15 and h01 == 1) else None)
        st.misc[("wqkv_pe", pb)] = tokqkv
        fs = i % 2
        P.wait("s", tokqkv)
        P.waits("s", scr_war)
        P.wait("s", st.misc.get(("fscr_free", fs)))
        P.wait("s", st.misc.get(("v2_free", s8)))
        PQ3 = PQ[:].rearrange("p (h n) -> p h n", n=512)
        P.op("s", lambda e, PQ3=PQ3, fs=fs: e.activation(
            q2f[fs].rearrange("p (h n) -> p h n", n=128), PQ3[:, :, 0:128], AF.Copy))
        P.op("s", lambda e, PQ3=PQ3, fs=fs: e.activation(
            k2f[fs].rearrange("p (h n) -> p h n", n=128), PQ3[:, :, 128:256], AF.Copy))
        tev = P.op("s", lambda e, PQ3=PQ3, s8=s8: e.activation(
            v2[s8].rearrange("p (h n) -> p h n", n=128), PQ3[:, :, 256:384], AF.Copy),
            inc="act")
        st.bank_free[bks[0]] = tev
        st.bank_free[bks[1]] = tev
        tevs[i] = tev
        P.wait("v", tev)
        P.wait("v", st.misc.get(("qrkr_free", s4)))
        q3 = q2f[fs].rearrange("p (h n) -> p h n", n=128)
        t3 = tmpq[fs].rearrange("p (h n) -> p h n", n=128)
        c3 = qcf[fs].rearrange("p (h n) -> p h n", n=128)
        P.op("v", lambda e, q3=q3, t3=t3: e.tensor_tensor(
            t3[:, :, 0:64], q3[:, :, 64:128],
            t.sinm_q[:, 0:64].unsqueeze(1).broadcast_to((128, 2, 64)), ALU.mult))
        P.op("v", lambda e, q3=q3, t3=t3: e.tensor_tensor(
            t3[:, :, 64:128], q3[:, :, 0:64],
            t.sinm_q[:, 64:128].unsqueeze(1).broadcast_to((128, 2, 64)), ALU.mult))
        P.op("v", lambda e, q3=q3, c3=c3: e.tensor_tensor(
            c3, q3, t.cos_q[:].unsqueeze(1).broadcast_to((128, 2, 128)), ALU.mult))
        tq = P.op("v", lambda e, s4=s4, fs=fs: e.tensor_add(qr[s4], qcf[fs], tmpq[fs]),
                  inc="dve")
        k3 = k2f[fs].rearrange("p (h n) -> p h n", n=128)
        kr3 = kr[s4].rearrange("p (h n) -> p h n", n=128)
        P.op("v", lambda e, k3=k3, kr3=kr3: e.tensor_tensor(
            kr3[:, :, 0:64], k3[:, :, 64:128],
            t.sinm_k[:, 0:64].unsqueeze(1).broadcast_to((128, 2, 64)), ALU.mult))
        P.op("v", lambda e, k3=k3, kr3=kr3: e.tensor_tensor(
            kr3[:, :, 64:128], k3[:, :, 0:64],
            t.sinm_k[:, 64:128].unsqueeze(1).broadcast_to((128, 2, 64)), ALU.mult))
        P.op("v", lambda e, k3=k3: e.tensor_tensor(
            k3, k3, t.cos_k[:].unsqueeze(1).broadcast_to((128, 2, 128)), ALU.mult))
        tk = P.op("v", lambda e, s4=s4, fs=fs: e.tensor_tensor(
            kr[s4], k2f[fs], kr[s4], ALU.add), inc="dve")
        st.misc[("fscr_free", fs)] = tk
        state[("rope", i)] = (tq, tk)

    def stageB1a(i):
        s4 = i % 4
        pair = (i + 1) % 2
        PQp = t.PBIG0 if pair == 0 else t.PBIG1
        pv = bf16v(PQp[:])
        dsts = (pv[:, 768:896], pv[:, 896:1024], pv[:, 1792:1920], pv[:, 1920:2048])
        tq, tk = state.pop(("rope", i))
        P.wait("t", tq)
        P.wait("t", tk)
        P.wait("t", tevs.get(i + 1))
        P.wait("t", st.misc.get(("qtb_prev", pair)))
        srcs = (qr[s4][:, 0:128], qr[s4][:, 128:256],
                kr[s4][:, 0:128], kr[s4][:, 128:256])
        ttp = None
        for j in range(4):
            ttp = P.op("t", lambda e, dst=dsts[j], src=srcs[j], j=j: e.matmul(
                dst, src, t.id_bf[:],
                is_transpose=True, start=(j == 0), stop=(j == 3)),
                inc="pe" if j == 3 else None)
        st.misc[("qrkr_free", s4)] = ttp
        P.wait("v", ttp)
        P.wait("v", st.misc.get(("qkT_free", s4)))
        P.op("v", lambda e, pv=pv, s4=s4: e.tensor_copy(
            qkT[s4][:, 0:256], pv[:, 768:1024]))
        tqk = P.op("v", lambda e, pv=pv, s4=s4: e.tensor_copy(
            qkT[s4][:, 256:512], pv[:, 1792:2048]), inc="dve")
        st.misc[("qtb_free", pair)] = tqk
        st.misc[("qtb_prev", pair)] = tqk
        state[("tqk", i)] = tqk

    def stageB1b(i):
        s4 = i % 4
        sb_ = 4 + i % 4
        tqk = state.pop(("tqk", i))
        P.wait("t", tqk)
        P.wait("t", st.misc.get(("mid_free", i % 4)))
        tsc = None
        for h01 in range(2):
            P.op("t", lambda e, s4=s4, h01=h01, sb_=sb_: e.matmul(
                t.BK[sb_][:, h01 * 128:(h01 + 1) * 128],
                qkT[s4][:, h01 * 128:(h01 + 1) * 128],
                qkT[s4][:, 256 + h01 * 128:256 + (h01 + 1) * 128],
                start=(h01 == 0), stop=False))
            tsc = P.op("t", lambda e, h01=h01, sb_=sb_: e.matmul(
                t.BK[sb_][:, h01 * 128:(h01 + 1) * 128], t.id_bf[:], t.mask_bf[:],
                start=False, stop=(h01 == 1)), inc="pe" if h01 == 1 else None)
        st.misc[("qkT_free", s4)] = tsc
        state[("tsc", i)] = tsc

    def stageB2a(i):
        s4 = i % 4
        par = i % 2
        sb_ = 4 + i % 4
        tsc = state.pop(("tsc", i))
        C = t.smallc[:, par * 256:(par + 1) * 256]
        negmax2 = C[:, 224:226]
        rs2 = C[:, 226:228]
        rec2 = C[:, 228:230]
        P.wait("v", tsc)
        P.wait("v", st.misc.get(("sm_free", par)))
        dnm = P.vs(lambda e, negmax2=negmax2, sb_=sb_: e.reduce_max(
            negmax2, t.BK[sb_].rearrange("p (h n) -> p h n", n=128)[:, 0:2, :],
            axis=AX.X, negate=True))
        P.wait("s", dnm)
        fs = i % 2
        P.wait("s", st.misc.get(("e2f_free", fs)))
        tse = None
        for h01 in range(2):
            tse = P.op("s", lambda e, h01=h01, fs=fs, sb_=sb_, negmax2=negmax2, rs2=rs2:
                       e.activation(
                           e2f[fs][:, h01 * 128:(h01 + 1) * 128],
                           t.BK[sb_][:, h01 * 128:(h01 + 1) * 128], AF.Exp,
                           bias=negmax2[:, h01:h01 + 1],
                           accum_out=rs2[:, h01:h01 + 1]),
                       inc="act" if h01 == 1 else None)
        st.misc[("sc_free", par)] = tse
        P.wait("v", tse)
        P.vs(lambda e, rec2=rec2, rs2=rs2: e.reciprocal(rec2, rs2))
        P.wait("v", st.misc.get(("pbf_free", s4)))
        tpn = None
        for h01 in range(2):
            tpn = P.op("v", lambda e, s4=s4, fs=fs, h01=h01, rec2=rec2:
                       e.tensor_scalar_mul(
                           pbf[s4][:, h01 * 128:(h01 + 1) * 128],
                           e2f[fs][:, h01 * 128:(h01 + 1) * 128],
                           rec2[:, h01:h01 + 1]), inc="dve" if h01 == 1 else None)
        st.misc[("sm_free", par)] = tpn
        st.misc[("e2f_free", fs)] = tpn
        state[("tpn", i)] = tpn

    def stageB2b1(i):
        s4 = i % 4
        sb_ = 4 + i % 4
        tpn = state.pop(("tpn", i))
        P.wait("t", tpn)
        pv = bf16v(t.BK[sb_])
        tpt = None
        for j in range(2):
            tpt = P.op("t", lambda e, pv=pv, j=j, s4=s4: e.matmul(
                pv[:, 512 + j * 128:512 + (j + 1) * 128],
                pbf[s4][:, j * 128:(j + 1) * 128],
                t.id_bf[:], is_transpose=True, start=(j == 0), stop=(j == 1)),
                inc="pe" if j == 1 else None)
        st.misc[("pbf_free", s4)] = tpt
        P.wait("v", tpt)
        P.wait("v", st.misc.get(("pT_free", s4)))
        tptc = P.op("v", lambda e, pv=pv, s4=s4: e.tensor_copy(pT[s4], pv[:, 512:768]),
                    inc="dve")
        state[("tptc", i)] = tptc

    def stageB2b2(i):
        hp, tt = i // NTILE, i % NTILE
        s4 = i % 4
        s8 = i % 8
        sb_ = 4 + i % 4
        tptc = state.pop(("tptc", i))
        P.wait("t", tptc)
        tcx = None
        for h01 in range(2):
            tcx = P.op("t", lambda e, s8=s8, s4=s4, h01=h01, sb_=sb_: e.matmul(
                t.BK[sb_][:, 256 + h01 * 128:256 + (h01 + 1) * 128],
                v2[s8][:, h01 * 128:(h01 + 1) * 128],
                pT[s4][:, h01 * 128:(h01 + 1) * 128],
                start=(h01 == 0), stop=(h01 == 1)), inc="pe" if h01 == 1 else None)
        st.misc[("pT_free", s4)] = tcx
        st.misc[("v2_free", s8)] = tcx
        P.wait("s", tcx)
        tcc = None
        for h01 in range(2):
            slot = (hp * 2 + h01) * 4 + tt
            tcc = P.op("s", lambda e, slot=slot, h01=h01, sb_=sb_: e.activation(
                ctxT3[:, slot, :], t.BK[sb_][:, 256 + h01 * 128:256 + (h01 + 1) * 128],
                AF.Copy), inc="act" if h01 == 1 else None)
        st.misc[("mid_free", i % 4)] = tcc
        st.bank_free[sb_] = tcc
        return tcc

    tcc = None
    for k in range(NIT + 6):
        if k >= 6:
            tcc = stageB2b2(k - 6)
        if 5 <= k < NIT + 5:
            stageB2b1(k - 5)
        if 4 <= k < NIT + 4:
            stageB2a(k - 4)
        if 3 <= k < NIT + 3:
            stageB1b(k - 3)
        if 2 <= k < NIT + 2:
            stageB1a(k - 2)
        if k < NIT:
            stageA(k)
    st.misc["ctxT_done"] = tcc
    st.misc["l1_pe_end"] = P.now("pe")
    st.big_r.add(P.now("pe"))
    st.scr_r.add(P.now("dve"))
    st.scr_r.add(P.now("pool"))
    st.scr_r.add(P.now("act"))


def emit_gh_pre(P, t, st, m):
    txa = {}
    xas = [bigbf(t, 31744 + tt * 128, 128) for tt in range(NTILE)]
    for tt in range(NTILE):
        par = tt % 2
        bk = 4 + par
        C = t.smallc[:, par * 256:(par + 1) * 256]
        logit8 = C[:, 168:176]
        u8 = C[:, 176:184]
        is8 = C[:, 184:192]
        msk8 = C[:, 192:200]
        m1n = C[:, 200:201]
        m12 = C[:, 201:202]
        es1 = C[:, 202:203]
        zc = C[:, 203:204]
        rZ = C[:, 204:205]
        k8a = C[:, 208:216]
        w8 = C[:, 216:224]
        e8 = C[:, 232:240]
        P.wait("t", st.xnT_ready)
        P.wait("t", st.misc.get("setup_g"))
        P.wait("t", st.bank_free[bk])
        for kc in range(KC):
            P.op("t", lambda e, kc=kc, tt=tt, bk=bk: e.matmul(
                t.BK[bk][:, 0:136],
                t.xnT[:, kc * TOK + tt * 128:kc * TOK + tt * 128 + 128],
                t.gwAT_bf[:, kc * 136:(kc + 1) * 136],
                start=(kc == 0), stop=False))
        tla = P.op("t", lambda e, bk=bk: e.matmul(
            t.BK[bk][:, 0:8], t.ones_bf[:], t.rows_bf[:, 4096:4104],
            start=False, stop=True), inc="pe")
        P.wait("v", tla)
        P.wait("v", st.misc.get(("gate_free", par)))
        P.vs(lambda e, logit8=logit8, bk=bk: e.tensor_copy(logit8, t.BK[bk][:, 0:8]))
        P.vs(lambda e, m1n=m1n, logit8=logit8: e.reduce_max(
            m1n, logit8, axis=AX.X, negate=True))
        P.vs(lambda e, u8=u8, logit8=logit8, m1n=m1n: e.tensor_scalar_add(
            u8, logit8, m1n))
        P.vs(lambda e, is8=is8, u8=u8: e.tensor_single_scalar(
            is8, u8, 0.0, ALU.is_ge))
        P.vs(lambda e, msk8=msk8, is8=is8, u8=u8: e.scalar_tensor_tensor(
            msk8, is8, -1e30, u8, ALU.mult, ALU.add))
        dm = P.vs(lambda e, m12=m12, msk8=msk8: e.reduce_max(
            m12, msk8, axis=AX.X, negate=True))
        P.wait("s", dm)
        P.op("s", lambda e, e8=e8, u8=u8: e.activation(e8, u8, AF.Exp))
        sa = P.op("s", lambda e, es1=es1, m12=m12: e.activation(
            es1, m12, AF.Exp, scale=-1.0), inc="act")
        P.wait("v", sa)
        P.vs(lambda e, zc=zc, es1=es1: e.tensor_scalar_add(zc, es1, 1.0))
        P.vs(lambda e, rZ=rZ, zc=zc: e.reciprocal(rZ, zc))
        P.vs(lambda e, k8a=k8a, u8=u8, m12=m12: e.tensor_scalar_add(k8a, u8, m12))
        P.vs(lambda e, k8a=k8a: e.tensor_single_scalar(k8a, k8a, 0.0, ALU.is_ge))
        P.vs(lambda e, w8=w8, e8=e8, k8a=k8a: e.tensor_mul(w8, e8, k8a))
        P.vs(lambda e, w8=w8, rZ=rZ: e.tensor_scalar_mul(w8, w8, rZ))
        tx = None
        for ex in range(8):
            tx = P.op("v", lambda e, ex=ex, xa=xas[tt], w8=w8, bk=bk:
                      e.tensor_scalar_mul(
                          xa[:, ex * 16:(ex + 1) * 16],
                          t.BK[bk][:, 8 + ex * 16:8 + (ex + 1) * 16],
                          w8[:, ex:ex + 1]), inc="dve" if ex == 7 else None)
        txa[tt] = tx
        st.bank_free[bk] = tx
        st.misc[("gate_free", par)] = tx
    for tt in range(NTILE):
        b = 6 + tt % 2
        P.wait("t", txa[tt])
        P.wait("t", st.bank_free[b])
        txt = P.op("t", lambda e, xa=xas[tt], b=b: e.matmul(
            bf16v(t.BK[b])[:, 0:128], xa, t.id_bf[:],
            is_transpose=True, start=True, stop=True), inc="pe")
        P.wait("v", txt)
        txc = P.op("v", lambda e, tt=tt, b=b: e.tensor_copy(
            t.xascT[:, tt * 128:(tt + 1) * 128], bf16v(t.BK[b])[:, 0:128]), inc="dve")
        st.bank_free[b] = txc
        st.misc[("xascT_tok", tt)] = txc

    if "GPM" in DBG_BARRIER:
        bar(P)

def emit_gh(P, t, st, m, next_src=None, nm=None):
    war = st.big_r.toks()
    st.big_r.clear()
    scr_war = st.scr_r.toks()
    st.scr_r.clear()
    WTb = [bigbf(t, 0, 8192), bigbf(t, 8192, 8192)]
    wdnb = [bigbf(t, 16384, 8192), bigbf(t, 24576, 8192)]
    osb = [t.scr32[:, 0:512], t.scr32[:, 512:1024]]
    qkv0 = st.misc.get(("wqkv_pe", 0))
    qkv1 = st.misc.get(("wqkv_pe", 1))
    l1end = st.misc.get("l1_pe_end")
    region_tok = {("WT", 0): (qkv0,), ("WT", 1): (qkv0, qkv1),
                  ("wdn", 0): (qkv1,), ("wdn", 1): (l1end,)}
    ghw_tok = {}

    def issue_ghw(oc):
        pb = oc % 2
        P.waits("y", region_tok[("WT", pb)])
        P.wait("y", st.misc.get(("WT_pe", pb)))
        P.wait("y", st.misc.get(("conv", "WT")))
        tok_WT = P.dma("y", "WT%d" % pb, lambda e, oc=oc, pb=pb: e.dma_start(
            WTb[pb].rearrange("p (k n) -> p k n", n=512),
            t.WTS.rearrange("(k p) n -> p k n", p=128)[:, :, oc * 512:(oc + 1) * 512]))
        P.waits("y", region_tok[("wdn", pb)])
        P.wait("y", st.misc.get(("wdn_pe", pb)))
        P.wait("y", st.misc.get(("conv", "wdn")))
        tok_wdn = P.dma("y", "wdn%d" % pb, lambda e, oc=oc, pb=pb: e.dma_start(
            wdnb[pb].rearrange("p (k n) -> p k n", n=512),
            t.wdnS.rearrange("(k p) n -> p k n", p=128)[:, :, oc * 512:(oc + 1) * 512]))
        ghw_tok[oc] = (tok_WT, tok_wdn)

    issue_ghw(0)
    issue_ghw(1)
    for oc in range(4):
        pb = oc % 2
        tok_WT, tok_wdn = ghw_tok.pop(oc)
        for tt in range(NTILE):
            par = (oc * NTILE + tt) % 2
            bk = 0 if par == 0 else 2
            P.wait("t", tok_WT)
            P.wait("t", tok_wdn)
            P.wait("t", st.misc.get("ctxT_done"))
            P.wait("t", st.misc.get(("xascT_tok", tt)))
            P.wait("t", st.bank_free[bk])
            for h in range(16):
                P.op("t", lambda e, h=h, tt=tt, pb=pb, bk=bk: e.matmul(
                    t.BK[bk], t.ctxT[:, (h * 4 + tt) * 128:(h * 4 + tt) * 128 + 128],
                    wdnb[pb][:, h * 512:(h + 1) * 512],
                    start=(h == 0), stop=False))
            for kc in range(KC):
                P.op("t", lambda e, kc=kc, tt=tt, pb=pb, bk=bk: e.matmul(
                    t.BK[bk], t.xnT[:, kc * TOK + tt * 128:kc * TOK + tt * 128 + 128],
                    WTb[pb][:, kc * 512:(kc + 1) * 512],
                    start=False, stop=False))
            P.op("t", lambda e, tt=tt, oc=oc, bk=bk: e.matmul(
                t.BK[bk], t.xascT[:, tt * 128:(tt + 1) * 128],
                t.ballT_bf[:, oc * 512:(oc + 1) * 512],
                start=False, stop=False))
            td = P.op("t", lambda e, oc=oc, bk=bk: e.matmul(
                t.BK[bk], t.ones_bf[:], t.rows_bf[:, 2048 + oc * 512:2048 + (oc + 1) * 512],
                start=False, stop=True), inc="pe")
            st.misc[("WT_pe", pb)] = td
            st.misc[("wdn_pe", pb)] = td
            P.wait("v", td)
            P.waits("v", scr_war)
            P.wait("v", st.misc.get(("osb_free", par)))
            tvo = P.op("v", lambda e, tt=tt, oc=oc, par=par, bk=bk: e.tensor_add(
                osb[par], t.BK[bk],
                t.x1[:, tt * D + oc * 512:tt * D + (oc + 1) * 512]), inc="dve")
            st.bank_free[bk] = tvo
            st.x1_r[tt].add(tvo)
            j0 = m * MJ + tt * 8
            P.wait("y", tvo)
            tst = P.dma("y", "os%d" % par, lambda e, j0=j0, oc=oc, par=par: e.dma_start(
                t.out.transpose([1, 0, 2])[j0:j0 + 8][:, :, oc * 512:(oc + 1) * 512],
                osb[par]))
            st.misc[("osb_free", par)] = tst
            if oc == 3 and next_src is not None:
                j0n = nm * MJ + tt * 8
                P.waits("y", st.x1_r[tt].toks())
                P.wait("y", st.x1_w[tt])
                tok = P.dma("y", "x%d" % tt, lambda e, j0n=j0n, tt=tt: e.dma_start(
                    t.x1[:, tt * D:(tt + 1) * D],
                    next_src.transpose([1, 0, 2])[j0n:j0n + 8]))
                st.x1_w[tt] = tok
                st.x1_r[tt].clear()
        if oc + 2 < 4:
            issue_ghw(oc + 2)
    st.big_r.add(P.now("pe"))
    st.scr_r.add(st.misc.get(("osb_free", 0)))
    st.scr_r.add(st.misc.get(("osb_free", 1)))
    st.scr_r.add(P.now("dve"))


def build(mode, dbg=0):
    layer0 = mode in ("l0", "fused")
    layer1 = mode in ("l1", "fused")
    nc = bass.Bass("TRN2", target_bir_lowering=False)
    t = T()
    es = contextlib.ExitStack()
    with es:
        _alloc(nc, es, t, layer0, layer1)
        if dbg:
            t.dbg_xnT = nc.dram_tensor("dbg_xnT", [dbg, 128, KC * TOK], BF16,
                                       kind="ExternalOutput").ap()
            t.dbg_x1b = nc.dram_tensor("dbg_x1b", [dbg, 128, NTILE * D], F32,
                                       kind="ExternalOutput").ap()
            t.dbg_fuse = nc.dram_tensor("dbg_fuse", [dbg, 128, 32768], BF16,
                                        kind="ExternalOutput").ap()
        P = Rec(nc, es)
        st = St()
        emit_setup(P, t, st, layer0, layer1)

        def dump(dst, src_ap, toks):
            P.waits("y", toks)
            return P.dma("y", "dbg", lambda e: e.dma_start(dst, src_ap))

        P.wait("t", st.misc.get("setup_g"))
        P.wait("t", st.misc.get("setup_y"))
        xsrc = t.x if layer0 else t.x1i
        for m in range(NMEGA):
            if layer0:
                if m == 0 or not layer1:
                    emit_x_load(P, t, st, m, t.x)
                emit_ln(P, t, st, m, "big")
                if dbg and m < dbg:
                    st.xnT_r.add(dump(t.dbg_xnT[m], t.xnT[:], [st.xnT_ready]))
                emit_attn0(P, t, st, m)
                if dbg and m < dbg:
                    tk = dump(t.dbg_x1b[m], t.x1[:],
                              [st.x1_w[i] for i in range(NTILE)])
                    for i in range(NTILE):
                        st.x1_r[i].add(tk)
                emit_ffn0(P, t, st, m)
                if m == 0 and layer1:
                    cv = st.misc["convert_fn"]
                    st.misc[("conv", "wqkv")] = cv(t.wqkv, t.wqkvS, 3 * D)
                    st.misc[("conv", "wdn")] = cv(t.wdense, t.wdnS, D)
                    st.misc[("conv", "WT")] = cv(t.WT, t.WTS, D)
                    st.misc["stage_free"] = st.misc["conv_out_tok"]
                if dbg and m < dbg:
                    st.big_r.add(dump(t.dbg_fuse[m], t.big[:],
                                      [st.fuse_ready[i] for i in range(NIC)]))
                if not layer1:
                    emit_x1_store(P, t, st, m)
            if layer1:
                if not layer0 and m == 0:
                    emit_x_load(P, t, st, m, t.x1i)
                emit_ln(P, t, st, m, "wC")
                if "EF" in DBG_BARRIER:
                    bar(P)
                emit_gh_pre(P, t, st, m)
                emit_l1attn(P, t, st, m)
                if "FG" in DBG_BARRIER:
                    bar(P)
                nxt = xsrc if m + 1 < NMEGA else None
                emit_gh(P, t, st, m, next_src=nxt, nm=m + 1)
        engsems = {"pe", "dve", "act", "pool"}
        for name in list(P.sems):
            if name not in engsems:
                P.q["y"].append(lambda e, P=P, n=name: e.wait_ge(P.sems[n], P.cnt[n]))
        with nc.Block() as block:
            @block.tensor
            def _(e):
                for fn in P.q["t"]:
                    fn(e)

            @block.vector
            def _(e):
                for fn in P.q["v"]:
                    fn(e)

            @block.scalar
            def _(e):
                for fn in P.q["s"]:
                    fn(e)

            @block.gpsimd
            def _(e):
                for fn in P.q["g"]:
                    fn(e)

            @block.sync
            def _(e):
                for fn in P.q["y"]:
                    fn(e)
    return nc


def _np32(a):
    return np.ascontiguousarray(np.asarray(a, dtype=np.float32))


def _rope_tables():
    dim = 128
    inv_freq = (1.0 / (np.float32(10000.0) **
                       (np.arange(0, dim, 2, dtype=np.float32) / np.float32(dim)))).astype(np.float32)
    tpos = np.arange(NI, dtype=np.float32)
    freqs = np.einsum("s,d->sd", tpos, inv_freq).astype(np.float32)
    emb = np.concatenate([freqs, freqs], axis=-1)
    cos = np.cos(emb).astype(np.float32)
    sin = np.sin(emb).astype(np.float32)
    cos_t = np.tile(cos, (8, 1)).astype(np.float32)
    sin_t = np.tile(sin, (8, 1)).astype(np.float32)
    sinm = sin_t.copy()
    sinm[:, 0:64] *= np.float32(-1.0)
    scale = np.float32(1.0 / math.sqrt(128.0))
    return (np.ascontiguousarray(cos_t * scale), np.ascontiguousarray(sinm * scale),
            np.ascontiguousarray(cos_t), np.ascontiguousarray(sinm))


def _mask_const():
    m = np.full((128, 128), -10000.0, dtype=np.float32)
    for jj in range(8):
        for s_ in range(16):
            m[jj * 16 + s_, jj * 16:jj * 16 + s_ + 1] = 0.0
    return m


def _common_inputs(inputs):
    d = {}
    d["Wq"] = _np32(inputs["L0_attn_Wq"])
    d["Wkv"] = _np32(inputs["L0_attn_Wkv"])
    d["wg8"] = np.ascontiguousarray(np.asarray(inputs["L0_ffn_w_gate"], dtype=np.int8))
    d["wu8"] = np.ascontiguousarray(np.asarray(inputs["L0_ffn_w_up"], dtype=np.int8))
    d["wd8"] = np.ascontiguousarray(np.asarray(inputs["L0_ffn_w_down"], dtype=np.int8))
    d["sg_pc"] = np.ascontiguousarray(_np32(inputs["L0_ffn_s_gate"]).reshape(64, 128).T)
    d["su_pc"] = np.ascontiguousarray(_np32(inputs["L0_ffn_s_up"]).reshape(64, 128).T)
    d["sd_r"] = _np32(inputs["L0_ffn_s_down"]).reshape(1, D)
    d["bkv_r"] = _np32(inputs["L0_attn_bkv"]).reshape(1, 512)
    d["wqkv"] = _np32(inputs["L1_attn_w_qkv"])
    d["wdense"] = _np32(inputs["L1_attn_w_dense"])
    d["WT"] = np.ascontiguousarray(_np32(inputs["L1_ffn_W"]).T)
    gw = _np32(inputs["L1_ffn_gate_w"])
    lA = _np32(inputs["L1_ffn_lora_A"]).reshape(128, D)
    d["gwAT"] = np.ascontiguousarray(np.concatenate([gw.T, lA.T], axis=1))
    lB = _np32(inputs["L1_ffn_lora_B"])
    d["ballT"] = np.ascontiguousarray(np.transpose(lB, (0, 2, 1)).reshape(128, D))
    cq, sq, ck, sk = _rope_tables()
    d["cosq"], d["sinmq"], d["cosk"], d["sinmk"] = cq, sq, ck, sk
    d["maskc"] = _mask_const()
    d["rows3"] = np.ascontiguousarray(np.concatenate([
        _np32(inputs["L0_attn_bq"]).reshape(-1),
        _np32(inputs["L1_ffn_b"]).reshape(-1),
        _np32(inputs["L1_ffn_gate_b"]).reshape(-1)]).reshape(1, 4104))
    d["identc"] = np.eye(128, dtype=np.float32)
    d["onesc"] = np.ones((1, 128), dtype=np.float32)
    return d


L0_KEYS = {"x", "Wq", "Wkv", "wg8", "wu8", "wd8", "sg_pc", "su_pc", "sd_r",
           "bkv_r", "rows3", "identc", "onesc"}
L1_KEYS = {"x1i", "wqkv", "wdense", "WT", "gwAT", "ballT", "cosq", "sinmq",
           "cosk", "sinmk", "maskc", "rows3", "identc", "onesc"}

MODE = "fused"


def kernel(**inputs):
    common = _common_inputs(inputs)
    x = _np32(inputs["L0_attn_x"])
    xs = [np.ascontiguousarray(x[:, c * JC:(c + 1) * JC, :]) for c in range(NCORES)]

    if MODE == "2launch":
        nc0 = build("l0")
        in0 = [{k: v for k, v in dict(common, x=xs[c]).items() if k in L0_KEYS}
               for c in range(NCORES)]
        r0 = run_bass_kernel_spmd(nc0, in0, core_ids=list(range(NCORES)))
        x1s = [r0.results[c]["x1o"] for c in range(NCORES)]
        nc1 = build("l1")
        in1 = [dict({k: v for k, v in common.items() if k in L1_KEYS}, x1i=x1s[c])
               for c in range(NCORES)]
        r1 = run_bass_kernel_spmd(nc1, in1, core_ids=list(range(NCORES)))
        outs = [r1.results[c]["out"] for c in range(NCORES)]
    else:
        ncf = build("fused")
        ins = [dict(common, x=xs[c]) for c in range(NCORES)]
        rf = run_bass_kernel_spmd(ncf, ins, core_ids=list(range(NCORES)))
        outs = [rf.results[c]["out"] for c in range(NCORES)]

    return np.concatenate(outs, axis=1)


# version 22 (final)
# speedup vs baseline: 1.1049x; 1.1049x over previous
import contextlib
import math
import numpy as np

import concourse.bass as bass
import concourse.mybir as mybir
from concourse.bass_utils import run_bass_kernel_spmd

F32 = mybir.dt.float32
BF16 = mybir.dt.bfloat16
I8 = mybir.dt.int8
AF = mybir.ActivationFunctionType
ALU = mybir.AluOpType
AX = mybir.AxisListType

NCORES = 8
NI = 16
SJ = 2048
JC = SJ // NCORES
D = 2048
KC = 16
NMEGA = 8
MJ = JC // NMEGA
NTILE = 4
TOK = 512
INTER = 8192
NIC = 64
EPS = 1e-5


class Rec:
    ENGS = ("t", "v", "s", "g", "y")

    def __init__(self, nc, es):
        self.nc = nc
        self.es = es
        self.q = {e: [] for e in self.ENGS}
        self.sems = {}
        self.cnt = {}
        self.lastw = {}
        for name in ("pe", "dve", "act", "pool"):
            self.sems[name] = es.enter_context(nc.semaphore(name))
            self.cnt[name] = 0

    def dsem(self, name):
        if name not in self.sems:
            self.sems[name] = self.es.enter_context(self.nc.semaphore(name))
            self.cnt[name] = 0
        return name

    def op(self, eng, fn, inc=None, by=1):
        if inc is None:
            self.q[eng].append(fn)
            return None
        self.cnt[inc] += by
        h = self.sems[inc]
        self.q[eng].append(lambda e, f=fn, h=h, b=by: f(e).then_inc(h, b))
        return (inc, self.cnt[inc])

    def dma(self, eng, sem, fn):
        return self.op(eng, fn, inc=self.dsem(sem), by=16)

    def wait(self, eng, tok):
        if tok is None:
            return
        sem, val = tok
        key = (eng, sem)
        if self.lastw.get(key, 0) >= val:
            return
        self.lastw[key] = val
        h = self.sems[sem]
        self.q[eng].append(lambda e, h=h, v=val: e.wait_ge(h, v))

    def waits(self, eng, toks):
        for t in toks:
            self.wait(eng, t)

    def now(self, sem):
        return (sem, self.cnt[sem])

    def vs(self, fn):
        tok = self.op("v", fn, inc="dve")
        self.wait("v", tok)
        return tok


class TokSet:

    def __init__(self):
        self.d = {}

    def add(self, tok):
        if tok is None:
            return
        s, v = tok
        if self.d.get(s, -1) < v:
            self.d[s] = v

    def toks(self):
        return [(s, v) for s, v in self.d.items()]

    def clear(self):
        self.d = {}


DBG_BARRIER = set()


def bar(P):
    toks = [P.now(sm) for sm in ("pe", "dve", "act", "pool")]
    for eng in ("t", "v", "s", "g", "y"):
        P.waits(eng, toks)


class T:
    pass


def _alloc(nc, es, t, layer0, layer1):
    sb = lambda name, shape, dt: es.enter_context(nc.sbuf_tensor(name, shape, dt))
    ps = lambda name, shape: es.enter_context(nc.psum_tensor(name, shape, F32))
    din = lambda name, shape, dt: nc.dram_tensor(name, shape, dt, kind="ExternalInput").ap()
    dout = lambda name, shape, dt: nc.dram_tensor(name, shape, dt, kind="ExternalOutput").ap()

    if layer0:
        t.x = din("x", [NI, JC, D], F32)
        t.Wq = din("Wq", [D, D], F32)
        t.Wkv = din("Wkv", [D, 512], F32)
        t.wg8 = din("wg8", [D, INTER], I8)
        t.wu8 = din("wu8", [D, INTER], I8)
        t.wd8 = din("wd8", [INTER, D], I8)
        t.sg_pc = din("sg_pc", [128, 64], F32)
        t.su_pc = din("su_pc", [128, 64], F32)
        t.sd_r = din("sd_r", [1, D], F32)
        t.bkv_r = din("bkv_r", [1, 512], F32)
    if layer1:
        t.wqkv = din("wqkv", [D, 3 * D], F32)
        t.wdense = din("wdense", [D, D], F32)
        t.WT = din("WT", [D, D], F32)
        t.gwAT = din("gwAT", [D, 136], F32)
        t.ballT = din("ballT", [128, D], F32)
        t.cosq = din("cosq", [128, 128], F32)
        t.sinmq = din("sinmq", [128, 128], F32)
        t.cosk = din("cosk", [128, 128], F32)
        t.sinmk = din("sinmk", [128, 128], F32)
        t.maskc = din("maskc", [128, 128], F32)
    t.rows3 = din("rows3", [1, 4104], F32)
    t.identc = din("identc", [128, 128], F32)
    t.onesc = din("onesc", [1, 128], F32)

    if layer0:
        t.wqS = nc.dram_tensor("wqS", [D, D], BF16, kind="Internal").ap()
        t.wkvS = nc.dram_tensor("wkvS", [D, 512], BF16, kind="Internal").ap()
    if layer1:
        t.wqkvS = nc.dram_tensor("wqkvS", [D, 3 * D], BF16, kind="Internal").ap()
        t.WTS = nc.dram_tensor("WTS", [D, D], BF16, kind="Internal").ap()
        t.wdnS = nc.dram_tensor("wdnS", [D, D], BF16, kind="Internal").ap()
    if layer0 and not layer1:
        t.x1o = dout("x1o", [NI, JC, D], F32)
    if layer1 and not layer0:
        t.x1i = din("x1i", [NI, JC, D], F32)
    if layer1:
        t.out = dout("out", [NI, JC, D], F32)

    t.x1 = sb("x1", [128, NTILE * D], F32)
    t.xnT = sb("xnT", [128, KC * TOK], BF16)
    t.big = sb("big", [128, 32768], BF16)
    t.ctxT = sb("ctxT", [128, 64 * 128], BF16)
    t.wC = sb("wC", [128, 12288], BF16)
    t.scr32 = sb("scr32", [128, 5120], F32)
    t.smallc = sb("smallc", [128, 768], F32)
    t.xascT = sb("xascT", [128, NTILE * 128], BF16)
    t.epsc = sb("epsc", [128, 1], F32)
    t.ones_bf = sb("ones_bf", [1, 128], BF16)
    t.id_bf = sb("id_bf", [128, 128], BF16)
    t.mask_bf = sb("mask_bf", [128, 128], BF16)
    t.rows_bf = sb("rows_bf", [1, 4104], BF16)
    t.cos_q = sb("cos_q", [128, 128], F32)
    t.sinm_q = sb("sinm_q", [128, 128], F32)
    t.cos_k = sb("cos_k", [128, 128], F32)
    t.sinm_k = sb("sinm_k", [128, 128], F32)
    t.sg = sb("sg", [128, 64], F32)
    t.su = sb("su", [128, 64], F32)
    t.bkv_bc = sb("bkv_bc", [128, 512], F32)
    t.sdown_bc = sb("sdown_bc", [128, D], F32)
    t.gwAT_bf = sb("gwAT_bf", [128, KC * 136], BF16)
    t.ballT_bf = sb("ballT_bf", [128, D], BF16)

    t.PBIG0 = ps("PBIG0", [128, 1024])
    t.PBIG1 = ps("PBIG1", [128, 1024])
    t.PS4 = ps("PS4", [128, 512])
    t.PS5 = ps("PS5", [128, 512])
    t.PS6 = ps("PS6", [128, 512])
    t.PS7 = ps("PS7", [128, 512])
    t.BK = [t.PBIG0[:, 0:512], t.PBIG0[:, 512:1024],
            t.PBIG1[:, 0:512], t.PBIG1[:, 512:1024],
            t.PS4[:], t.PS5[:], t.PS6[:], t.PS7[:]]


def bf16v(ap):
    return ap.bitcast(BF16)


def big32(t, off, n):
    return t.big[:].bitcast(F32)[:, off:off + n]


def bigbf(t, off, n):
    return t.big[:, off:off + n]


class St:
    def __init__(self):
        self.bank_free = [None] * 8
        self.x1_w = [None] * NTILE
        self.x1_r = [TokSet() for _ in range(NTILE)]
        self.xnT_ready = None
        self.xnT_r = TokSet()
        self.fuse_ready = [None] * NIC
        self.big_w = TokSet()
        self.big_r = TokSet()
        self.wC_r = TokSet()
        self.scr_r = TokSet()
        self.misc = {}


def emit_setup(P, t, st, layer0, layer1):
    if layer0:
        P.dma("y", "ys", lambda e: e.dma_start(t.sg[:], t.sg_pc))
        P.dma("y", "ys", lambda e: e.dma_start(t.su[:], t.su_pc))
        P.dma("y", "ys", lambda e: e.dma_start(t.scr32[0:1, 0:512], t.bkv_r))
        P.dma("y", "ys", lambda e: e.dma_start(t.scr32[0:1, 512:2560], t.sd_r))
    if layer1:
        P.dma("y", "ys", lambda e: e.dma_start(t.cos_q[:], t.cosq))
        P.dma("y", "ys", lambda e: e.dma_start(t.sinm_q[:], t.sinmq))
        P.dma("y", "ys", lambda e: e.dma_start(t.cos_k[:], t.cosk))
        P.dma("y", "ys", lambda e: e.dma_start(t.sinm_k[:], t.sinmk))
    ytok = P.dma("y", "ys", lambda e: e.dma_start(t.scr32[0:1, 2560:2688], t.onesc))
    P.dma("g", "gs", lambda e: e.dma_start(t.id_bf[:], t.identc))
    P.dma("g", "gs", lambda e: e.dma_start(t.ones_bf[:], t.onesc))
    gtok = P.dma("g", "gs", lambda e: e.dma_start(t.rows_bf[:], t.rows3))
    if layer1:
        P.dma("g", "gs", lambda e: e.dma_start(t.mask_bf[:], t.maskc))
        P.dma("g", "gs", lambda e: e.dma_start(
            t.gwAT_bf[:].rearrange("p (k n) -> p k n", n=136),
            t.gwAT.rearrange("(k p) n -> p k n", p=128)))
        gtok = P.dma("g", "gs", lambda e: e.dma_start(t.ballT_bf[:], t.ballT))
    st.misc["setup_g"] = gtok
    st.misc["setup_y"] = ytok
    st.misc["eps_ready"] = P.op("v", lambda e: e.memset(t.epsc[:], EPS), inc="dve")

    stage = [t.ctxT[:, i * 2048:(i + 1) * 2048] for i in range(4)]
    out_tok = [None] * 4
    ci = [0]

    def convert(src, dst, ncols):
        last = None
        for kc in range(KC):
            for b0 in range(0, ncols, 2048):
                w = min(2048, ncols - b0)
                pb = ci[0] % 4
                ci[0] += 1
                P.wait("g", out_tok[pb])
                ti = P.dma("g", "cv%d" % pb, lambda e, kc=kc, b0=b0, w=w, pb=pb,
                           src=src: e.dma_start(
                               stage[pb][:, 0:w],
                               src.rearrange("(k p) n -> p k n", p=128)
                               [:, kc, b0:b0 + w]))
                P.wait("y", ti)
                last = P.dma("y", "cvo%d" % pb, lambda e, kc=kc, b0=b0, w=w, pb=pb,
                             dst=dst: e.dma_start(
                                 dst.rearrange("(k p) n -> p k n", p=128)
                                 [:, kc, b0:b0 + w],
                                 stage[pb][:, 0:w]))
                out_tok[pb] = last
        return tuple(tok for tok in out_tok if tok is not None)

    st.misc["convert_fn"] = convert
    st.misc["conv_out_tok"] = out_tok
    if layer0:
        st.misc[("conv", "wkv")] = convert(t.Wkv, t.wkvS, 512)
        st.misc[("conv", "wq")] = convert(t.Wq, t.wqS, D)
    if layer1 and not layer0:
        st.misc[("conv", "wqkv")] = convert(t.wqkv, t.wqkvS, 3 * D)
        st.misc[("conv", "wdn")] = convert(t.wdense, t.wdnS, D)
        st.misc[("conv", "WT")] = convert(t.WT, t.WTS, D)
    st.misc["stage_free"] = out_tok

    if layer0:
        ones32 = t.scr32[0:1, 2560:2688]
        P.wait("t", ytok)
        tok = P.op("t", lambda e: e.matmul(t.BK[4], ones32, t.scr32[0:1, 0:512],
                                           start=True, stop=True), inc="pe")
        P.wait("v", tok)
        tok2 = P.op("v", lambda e: e.tensor_copy(t.bkv_bc[:], t.BK[4]), inc="dve")
        st.bank_free[4] = tok2
        for oc in range(4):
            P.wait("t", st.bank_free[5])
            tok = P.op("t", lambda e, oc=oc: e.matmul(
                t.BK[5], ones32, t.scr32[0:1, 512 + oc * 512:512 + (oc + 1) * 512],
                start=True, stop=True), inc="pe")
            P.wait("v", tok)
            tok2 = P.op("v", lambda e, oc=oc: e.tensor_copy(
                t.sdown_bc[:, oc * 512:(oc + 1) * 512], t.BK[5]), inc="dve")
            st.bank_free[5] = tok2
        st.scr_r.add(P.now("pe"))


def emit_x_load(P, t, st, m, src):
    for tt in range(NTILE):
        j0 = m * MJ + tt * 8
        P.waits("y", st.x1_r[tt].toks())
        P.wait("y", st.x1_w[tt])
        tok = P.dma("y", "x%d" % tt, lambda e, j0=j0, tt=tt: e.dma_start(
            t.x1[:, tt * D:(tt + 1) * D], src.transpose([1, 0, 2])[j0:j0 + 8]))
        st.x1_w[tt] = tok
        st.x1_r[tt].clear()


def emit_ln(P, t, st, m, carve):
    if carve == "big":
        scr = big32(t, 0, 2048)
        xn_tm = [bigbf(t, 4096 + p * 2048, 2048) for p in (0, 1)]
        region = st.big_r
    else:
        wc32 = t.wC[:].bitcast(F32)
        scr = wc32[:, 0:2048]
        xn_tm = [t.wC[:, 4096 + p * 2048:4096 + (p + 1) * 2048] for p in (0, 1)]
        region = st.wC_r
    war = region.toks()
    region.clear()

    R = {}
    for tt in range(NTILE):
        S = t.smallc[:, 640 + tt * 8:648 + tt * 8]
        R[tt] = tuple(S[:, i:i + 1] for i in range(8))
    a1t = {}
    drt = {}
    d1t = {}
    a2t = {}
    stats = {}
    for tt in range(NTILE):
        s2 = R[tt][0]
        stats_free = st.misc.get(("ln_stats_free", tt))
        P.wait("s", st.x1_w[tt])
        P.waits("s", war)
        P.wait("s", stats_free)
        if tt > 0:
            P.wait("s", a1t[tt - 1])
        a1t[tt] = P.op("s", lambda e, tt=tt, scr=scr, s2=s2: e.activation(
            scr, t.x1[:, tt * D:(tt + 1) * D], AF.Square, accum_out=s2), inc="act")
        P.wait("v", st.x1_w[tt])
        P.wait("v", stats_free)
        drt[tt] = P.op("v", lambda e, tt=tt, s1=R[tt][1]: e.reduce_sum(
            s1, t.x1[:, tt * D:(tt + 1) * D], axis=AX.X), inc="dve")
    for tt in range(NTILE):
        s2, s1, mean, ex2, nvar = R[tt][0:5]
        P.wait("v", drt[tt])
        P.vs(lambda e, mean=mean, s1=s1: e.tensor_scalar_mul(mean, s1, 1.0 / D))
        P.wait("v", a1t[tt])
        P.vs(lambda e, ex2=ex2, s2=s2: e.tensor_scalar_mul(ex2, s2, 1.0 / D))
        d1t[tt] = P.op("v", lambda e, nvar=nvar, mean=mean, ex2=ex2:
                       e.scalar_tensor_tensor(
                           nvar, mean, mean, ex2, ALU.mult, ALU.subtract), inc="dve")
    for tt in range(NTILE):
        nvar, stdc = R[tt][4], R[tt][5]
        P.wait("s", d1t[tt])
        P.wait("s", st.misc.get("eps_ready"))
        a2t[tt] = P.op("s", lambda e, stdc=stdc, nvar=nvar: e.activation(
            stdc, nvar, AF.Sqrt, scale=-1.0, bias=t.epsc[:]), inc="act")
    for tt in range(NTILE):
        mean, stdc, rstd, nmr = R[tt][2], R[tt][5], R[tt][6], R[tt][7]
        P.wait("v", a2t[tt])
        P.vs(lambda e, rstd=rstd, stdc=stdc: e.reciprocal(rstd, stdc))
        d2 = P.op("v", lambda e, nmr=nmr, mean=mean, rstd=rstd: e.tensor_scalar(
            nmr, mean, rstd, -1.0, ALU.mult, ALU.mult), inc="dve")
        P.wait("v", d2)
        stats[tt] = (rstd, nmr, d2)

    xn_tok = [None, None]
    evac_last = None
    for tt in range(NTILE):
        par = tt % 2
        rstd, nmr, d2 = stats[tt]
        P.wait("s", d2)
        P.wait("s", xn_tok[par])
        a3 = P.op("s", lambda e, tt=tt, par=par, rstd=rstd, nmr=nmr: e.activation(
            xn_tm[par], t.x1[:, tt * D:(tt + 1) * D], AF.Identity, scale=rstd, bias=nmr),
            inc="act")
        st.misc[("ln_stats_free", tt)] = a3
        st.x1_r[tt].add(a3)
        st.x1_r[tt].add(d2)
        for half in range(2):
            bank = 6 + half
            P.wait("t", a3)
            P.wait("t", st.bank_free[bank])
            ptile = bf16v(t.BK[bank])
            tok = None
            for j in range(8):
                kc = half * 8 + j
                tok = P.op("t", lambda e, ptile=ptile, j=j, kc=kc, par=par: e.matmul(
                    ptile[:, j * 128:(j + 1) * 128],
                    xn_tm[par][:, kc * 128:(kc + 1) * 128], t.id_bf[:],
                    is_transpose=True, start=(j == 0), stop=(j == 7)),
                    inc="pe" if j == 7 else None)
            if half == 1:
                xn_tok[par] = tok
            region.add(tok)
            P.wait("v", tok)
            P.waits("v", st.xnT_r.toks())
            ev = P.op("v", lambda e, ptile=ptile, half=half, tt=tt: e.tensor_copy(
                t.xnT[:].rearrange("p (k n) -> p k n", n=TOK)
                [:, half * 8:(half + 1) * 8, tt * 128:(tt + 1) * 128],
                ptile[:].rearrange("p (k n) -> p k n", n=128)), inc="dve")
            st.bank_free[bank] = ev
            evac_last = ev
    region.add(P.now("act"))
    st.xnT_ready = evac_last


def emit_attn0(P, t, st, m):
    wkv = t.wC[:, 0:8192]
    wqb = [bigbf(t, 16384, 8192), bigbf(t, 24576, 8192)]
    war = st.big_r.toks()
    st.big_r.clear()
    P.waits("g", st.wC_r.toks())
    P.waits("g", st.misc.get(("conv", "wkv"), ()))
    tok_wkv = P.dma("g", "wkv", lambda e: e.dma_start(
        wkv.rearrange("p (k n) -> p k n", n=512),
        t.wkvS.rearrange("(k p) n -> p k n", p=128)))

    kvsb = [t.scr32[:, tt * 512:(tt + 1) * 512] for tt in range(NTILE)]
    prod = [t.scr32[:, 2048:2560], t.scr32[:, 2560:3072]]
    s_all = [t.smallc[:, 512 + tt * 32:512 + (tt + 1) * 32] for tt in range(NTILE)]
    scr_war = st.scr_r.toks()
    st.scr_r.clear()

    tok_kv = [None] * NTILE
    for tt in range(NTILE):
        P.wait("t", tok_wkv)
        P.wait("t", st.xnT_ready)
        P.wait("t", st.bank_free[4])
        tokkv = None
        for kc in range(KC):
            tokkv = P.op("t", lambda e, kc=kc, tt=tt: e.matmul(
                t.BK[4], t.xnT[:, kc * TOK + tt * 128:kc * TOK + tt * 128 + 128],
                wkv.rearrange("p (k n) -> p k n", n=512)[:, kc, :],
                start=(kc == 0), stop=(kc == 15)), inc="pe" if kc == 15 else None)
        P.wait("v", tokkv)
        P.waits("v", scr_war)
        P.wait("v", st.misc.get("kv_free"))
        tok_kv[tt] = P.op("v", lambda e, tt=tt: e.tensor_add(
            kvsb[tt], t.BK[4], t.bkv_bc[:]), inc="dve")
        st.bank_free[4] = tok_kv[tt]
    st.misc["wkv_pe"] = tokkv
    st.wC_r.add(tokkv)

    for g4 in range(4):
        pb = g4 % 2
        if g4 == 0:
            P.waits("g", war)
        P.wait("g", st.misc.get(("wq_pe", pb)))
        P.waits("g", st.misc.get(("conv", "wq"), ()))
        tok_wq = P.dma("g", "wq%d" % pb, lambda e, g4=g4, pb=pb: e.dma_start(
            wqb[pb].rearrange("p (k n) -> p k n", n=512),
            t.wqS.rearrange("(k p) n -> p k n", p=128)[:, :, g4 * 512:(g4 + 1) * 512]))
        for tt in range(NTILE):
            par = (g4 * NTILE + tt) % 2
            qb = par % 2
            P.wait("t", tok_wq)
            P.wait("t", st.bank_free[qb])
            for kc in range(KC):
                P.op("t", lambda e, kc=kc, tt=tt, pb=pb, qb=qb: e.matmul(
                    t.BK[qb], t.xnT[:, kc * TOK + tt * 128:kc * TOK + tt * 128 + 128],
                    wqb[pb].rearrange("p (k n) -> p k n", n=512)[:, kc, :],
                    start=(kc == 0), stop=False))
            tokq = P.op("t", lambda e, g4=g4, qb=qb: e.matmul(
                t.BK[qb], t.ones_bf[:], t.rows_bf[:, g4 * 512:(g4 + 1) * 512],
                start=False, stop=True), inc="pe")
            st.misc[("wq_pe", pb)] = tokq
            P.wait("v", tokq)
            P.wait("v", tok_kv[tt])
            P.wait("v", st.misc.get(("prod_free", par)))
            P.op("v", lambda e, qb=qb, par=par, g4=g4, tt=tt: e.tensor_tensor(
                prod[par].rearrange("p (h w) -> p h w", w=64),
                t.BK[qb].rearrange("p (h w) -> p h w", w=64),
                kvsb[tt][:, g4 * 128:g4 * 128 + 64].unsqueeze(1).broadcast_to((128, 8, 64)),
                ALU.mult))
            tokp = P.op("v", lambda e, par=par, g4=g4, tt=tt: e.reduce_sum(
                s_all[tt][:, g4 * 8:(g4 + 1) * 8],
                prod[par].rearrange("p (h w) -> p h w", w=64), axis=AX.X), inc="dve")
            st.bank_free[qb] = tokp
            st.misc[("prod_free", par)] = tokp
            st.misc[("s_red", tt)] = tokp

    tokx = None
    for tt in range(NTILE):
        par = tt % 2
        C = t.smallc[:, par * 256:(par + 1) * 256]
        z32, e32, at32, r32 = (C[:, 8 + 32 * i:8 + 32 * (i + 1)] for i in range(4))
        P.wait("v", st.misc.get(("attn_free", par)))
        P.wait("v", st.misc.get(("s_red", tt)))
        P.vs(lambda e, tt=tt, z32=z32: e.tensor_scalar_mul(z32, s_all[tt], 0.125))
        dz = P.vs(lambda e, z32=z32: e.tensor_sub(z32, z32, z32))
        P.wait("s", dz)
        ae = P.op("s", lambda e, e32=e32, z32=z32: e.activation(e32, z32, AF.Exp), inc="act")
        P.wait("v", ae)
        P.vs(lambda e, r32=r32, e32=e32: e.reciprocal(r32, e32))
        da = P.vs(lambda e, at32=at32, e32=e32, r32=r32: e.tensor_mul(at32, e32, r32))
        P.waits("v", st.x1_r[tt].toks())
        P.wait("v", st.x1_w[tt])
        for g4 in range(4):
            for h in range(8):
                c0 = g4 * 512 + h * 64
                tokx = P.op("v", lambda e, tt=tt, g4=g4, c0=c0, at32=at32, h=h:
                            e.scalar_tensor_tensor(
                                t.x1[:, tt * D + c0:tt * D + c0 + 64],
                                kvsb[tt][:, g4 * 128 + 64:(g4 + 1) * 128],
                                at32[:, g4 * 8 + h:g4 * 8 + h + 1],
                                t.x1[:, tt * D + c0:tt * D + c0 + 64],
                                ALU.mult, ALU.add),
                            inc="dve" if (g4 == 3 and h == 7) else None)
        st.x1_w[tt] = tokx
        st.x1_r[tt].clear()
        st.misc[("attn_free", par)] = tokx
    st.misc["kv_free"] = tokx
    st.big_r.add(P.now("pe"))
    st.scr_r.add(P.now("dve"))
    st.scr_r.add(tokx)


def emit_ffn0(P, t, st, m):
    war = st.big_r.toks()
    st.big_r.clear()
    scr_war = st.scr_r.toks()
    st.scr_r.clear()
    wgb = [t.wC[:, 0:2048], t.wC[:, 2048:4096]]
    wub = [t.wC[:, 4096:6144], t.wC[:, 6144:8192]]
    wdb = [t.wC[:, 8192:10240], t.wC[:, 10240:12288]]
    silu = [t.scr32[:, 0:512], t.scr32[:, 512:1024]]
    tmpd = [t.scr32[:, 1024:1536], t.scr32[:, 1536:2048]]
    wc_war = st.wC_r.toks()
    st.wC_r.clear()

    for ic in range(NIC):
        pb = ic % 2
        bg = pb
        bu = 2 + pb
        P.waits("g", wc_war)
        P.wait("g", st.misc.get(("wg_pe", pb)))
        tok_wg = P.dma("g", "wg%d" % pb, lambda e, ic=ic, pb=pb: e.dma_start(
            wgb[pb].rearrange("p (k n) -> p k n", n=128),
            t.wg8.rearrange("(k p) n -> p k n", p=128)[:, :, ic * 128:(ic + 1) * 128]))
        P.wait("g", st.misc.get(("wu_pe", pb)))
        tok_wu = P.dma("g", "wu%d" % pb, lambda e, ic=ic, pb=pb: e.dma_start(
            wub[pb].rearrange("p (k n) -> p k n", n=128),
            t.wu8.rearrange("(k p) n -> p k n", p=128)[:, :, ic * 128:(ic + 1) * 128]))
        P.wait("t", tok_wg)
        P.wait("t", st.xnT_ready)
        P.wait("t", st.bank_free[bg])
        tokg = None
        for kc in range(KC):
            tokg = P.op("t", lambda e, kc=kc, pb=pb, bg=bg: e.matmul(
                t.BK[bg], wgb[pb][:, kc * 128:(kc + 1) * 128],
                t.xnT[:, kc * TOK:(kc + 1) * TOK],
                start=(kc == 0), stop=(kc == 15)), inc="pe" if kc == 15 else None)
        st.misc[("wg_pe", pb)] = tokg
        P.wait("t", tok_wu)
        P.wait("t", st.bank_free[bu])
        toku = None
        for kc in range(KC):
            toku = P.op("t", lambda e, kc=kc, pb=pb, bu=bu: e.matmul(
                t.BK[bu], wub[pb][:, kc * 128:(kc + 1) * 128],
                t.xnT[:, kc * TOK:(kc + 1) * TOK],
                start=(kc == 0), stop=(kc == 15)), inc="pe" if kc == 15 else None)
        st.misc[("wu_pe", pb)] = toku
        P.wait("s", tokg)
        P.waits("s", scr_war)
        P.wait("s", st.misc.get("setup_y"))
        P.wait("s", st.misc.get(("silu_free", pb)))
        tsil = P.op("s", lambda e, ic=ic, pb=pb, bg=bg: e.activation(
            silu[pb], t.BK[bg], AF.Silu, scale=t.sg[:, ic:ic + 1]), inc="act")
        st.bank_free[bg] = tsil
        P.wait("v", toku)
        P.wait("v", tsil)
        if ic < 2:
            P.waits("v", war)
            P.waits("v", scr_war)
            P.wait("v", st.misc.get("setup_y"))
        tfu = P.op("v", lambda e, ic=ic, pb=pb, bu=bu: e.scalar_tensor_tensor(
            bigbf(t, ic * 512, 512), t.BK[bu], t.su[:, ic:ic + 1], silu[pb],
            ALU.mult, ALU.mult), inc="dve")
        st.bank_free[bu] = tfu
        st.misc[("silu_free", pb)] = tfu
        st.fuse_ready[ic] = tfu

    wdslot = [t.wC[:, i * 2048:(i + 1) * 2048] for i in range(6)]
    slot_tok = {0: st.misc.get(("wg_pe", 0)), 1: st.misc.get(("wg_pe", 1)),
                2: st.misc.get(("wu_pe", 0)), 3: st.misc.get(("wu_pe", 1)),
                4: st.misc.get(("wd_pe", 0)), 5: st.misc.get(("wd_pe", 1))}
    wd_tok = {}

    def issue_wd(icg):
        ic = icg % NIC
        sl = (icg + 4) % 6
        P.wait("g", slot_tok.get(sl))
        P.waits("g", wc_war)
        wd_tok[icg] = (sl, P.dma("g", "wd%d" % sl, lambda e, ic=ic, sl=sl:
                               e.dma_start(wdslot[sl], t.wd8[ic * 128:(ic + 1) * 128, :])))

    issue_wd(0)
    issue_wd(1)
    for half in range(2):
        tsubs = (2 * half, 2 * half + 1)
        bank_tok = [None] * 8
        for ic in range(NIC):
            icg = half * NIC + ic
            if icg + 2 < 2 * NIC:
                issue_wd(icg + 2)
            sl, tok_wd = wd_tok.pop(icg)
            P.wait("t", tok_wd)
            P.wait("t", st.fuse_ready[ic])
            tk = None
            for ti, tsub in enumerate(tsubs):
                for oc in range(4):
                    bk = ti * 4 + oc
                    if ic == 0:
                        P.wait("t", st.bank_free[bk])
                    tk = P.op("t", lambda e, ic=ic, tsub=tsub, oc=oc, sl=sl, bk=bk:
                              e.matmul(
                                  t.BK[bk], bigbf(t, ic * 512 + tsub * 128, 128),
                                  wdslot[sl][:, oc * 512:(oc + 1) * 512],
                                  start=(ic == 0), stop=(ic == NIC - 1)),
                              inc="pe" if (ic == NIC - 1 or (ti == 1 and oc == 3))
                              else None)
                    if ic == NIC - 1:
                        bank_tok[bk] = tk
            slot_tok[sl] = tk
        for ti, tsub in enumerate(tsubs):
            for oc in range(4):
                bk = ti * 4 + oc
                par = oc % 2
                P.wait("v", bank_tok[bk])
                P.wait("v", st.misc.get(("tmpd_free", par)))
                P.op("v", lambda e, bk=bk, oc=oc, par=par: e.tensor_mul(
                    tmpd[par], t.BK[bk], t.sdown_bc[:, oc * 512:(oc + 1) * 512]))
                P.waits("v", st.x1_r[tsub].toks())
                P.wait("v", st.x1_w[tsub])
                tokx = P.op("v", lambda e, tsub=tsub, oc=oc, par=par: e.tensor_add(
                    t.x1[:, tsub * D + oc * 512:tsub * D + (oc + 1) * 512],
                    t.x1[:, tsub * D + oc * 512:tsub * D + (oc + 1) * 512],
                    tmpd[par]), inc="dve")
                st.bank_free[bk] = tokx
                st.misc[("tmpd_free", par)] = tokx
                st.x1_w[tsub] = tokx
                st.x1_r[tsub].clear()
    st.misc[("wd_pe", 0)] = slot_tok[4]
    st.misc[("wd_pe", 1)] = slot_tok[5]
    st.misc[("wg_pe", 0)] = slot_tok[0]
    st.misc[("wg_pe", 1)] = slot_tok[1]
    st.misc[("wu_pe", 0)] = slot_tok[2]
    st.misc[("wu_pe", 1)] = slot_tok[3]
    st.big_r.add(P.now("pe"))
    st.wC_r.add(P.now("pe"))
    st.scr_r.add(P.now("dve"))
    st.scr_r.add(P.now("act"))


def emit_x1_store(P, t, st, m):
    for tt in range(NTILE):
        j0 = m * MJ + tt * 8
        P.wait("y", st.x1_w[tt])
        tok = P.dma("y", "xs%d" % tt, lambda e, j0=j0, tt=tt: e.dma_start(
            t.x1o.transpose([1, 0, 2])[j0:j0 + 8], t.x1[:, tt * D:(tt + 1) * D]))
        st.x1_r[tt].add(tok)


def emit_l1attn(P, t, st, m):
    war = st.big_r.toks()
    st.big_r.clear()
    scr_war = st.scr_r.toks()
    st.scr_r.clear()
    wqkvb = [bigbf(t, 0, 12288), bigbf(t, 12288, 12288)]
    FB = 24576
    qr = [bigbf(t, FB + s * 256, 256) for s in range(4)]
    kr = [bigbf(t, FB + 1024 + s * 256, 256) for s in range(4)]
    v2 = [bigbf(t, FB + 2048 + s * 256, 256) for s in range(8)]
    qkT = [bigbf(t, FB + 4096 + s * 512, 512) for s in range(4)]
    pbf = [bigbf(t, FB + 6144 + s * 256, 256) for s in range(4)]
    pT = [bigbf(t, FB + 7168 + s * 256, 256) for s in range(4)]
    q2f = [t.scr32[:, sl * 1280:sl * 1280 + 256] for sl in range(4)]
    k2f = [t.scr32[:, sl * 1280 + 256:sl * 1280 + 512] for sl in range(4)]
    tmpq = [t.scr32[:, sl * 1280 + 512:sl * 1280 + 768] for sl in range(4)]
    qcf = [t.scr32[:, sl * 1280 + 768:sl * 1280 + 1024] for sl in range(4)]
    e2f = [t.scr32[:, sl * 1280 + 1024:sl * 1280 + 1280] for sl in range(4)]

    ctxT3 = t.ctxT[:].rearrange("p (s n) -> p s n", n=128)
    for tok in st.misc.get("stage_free", ()):
        P.wait("s", tok)
    P.waits("s", war)
    P.wait("v", st.misc.get("setup_y"))
    P.wait("g", st.misc.get("setup_y"))
    for s in range(4):
        st.misc[("mid_free", s)] = st.bank_free[4 + s]

    pre_wq = {}
    for hp in range(2):
        P.waits("y", war)
        P.wait("y", st.misc.get(("wqkv_pe", hp)))
        P.waits("y", st.misc.get(("conv", "wqkv"), ()))
        pre_wq[hp] = P.dma("y", "wqkvy%d" % hp, lambda e, hp=hp: e.dma_start(
            wqkvb[hp].rearrange("p (k n) -> p k n", n=768),
            t.wqkvS.rearrange("(k p) n -> p k n", p=128)
            [:, :, hp * 768:(hp + 1) * 768]))

    NIT = 8 * NTILE
    state = {}
    tevs = {}
    tok_wq = [None]

    def stageA(i):
        hp, tt = i // NTILE, i % NTILE
        pb = hp % 2
        s4 = i % 4
        s8 = i % 8
        if tt == 0:
            if hp < 2:
                tok_wq[0] = pre_wq[hp]
            else:
                P.wait("g", st.misc.get(("wqkv_pe", pb)))
                P.waits("g", st.misc.get(("conv", "wqkv"), ()))
                tok_wq[0] = P.dma("g", "wqkv%d" % pb, lambda e, hp=hp, pb=pb:
                                  e.dma_start(
                    wqkvb[pb].rearrange("p (k n) -> p k n", n=768),
                    t.wqkvS.rearrange("(k p) n -> p k n", p=128)
                    [:, :, hp * 768:(hp + 1) * 768]))
        PQ = t.PBIG0 if i % 2 == 0 else t.PBIG1
        bks = (0, 1) if i % 2 == 0 else (2, 3)
        P.wait("t", tok_wq[0])
        P.wait("t", st.xnT_ready)
        P.wait("t", st.bank_free[bks[0]])
        P.wait("t", st.bank_free[bks[1]])
        P.wait("t", st.misc.get(("qtb_free", i % 2)))
        tokqkv = None
        for kc in range(KC):
            for h01 in range(2):
                tokqkv = P.op("t", lambda e, kc=kc, tt=tt, h01=h01, pb=pb, PQ=PQ:
                              e.matmul(
                                  PQ[:, h01 * 512:h01 * 512 + 384],
                                  t.xnT[:, kc * TOK + tt * 128:kc * TOK + tt * 128 + 128],
                                  wqkvb[pb].rearrange("p (k n) -> p k n", n=768)
                                  [:, kc, h01 * 384:(h01 + 1) * 384],
                                  start=(kc == 0), stop=(kc == 15)),
                              inc="pe" if (kc == 15 and h01 == 1) else None)
        st.misc[("wqkv_pe", pb)] = tokqkv
        P.wait("s", tokqkv)
        P.waits("s", scr_war)
        P.wait("s", st.misc.get(("fscr_free", s4)))
        P.wait("s", st.misc.get(("v2_free", s8)))
        PQ3 = PQ[:].rearrange("p (h n) -> p h n", n=512)
        P.op("s", lambda e, PQ3=PQ3, s4=s4: e.activation(
            q2f[s4].rearrange("p (h n) -> p h n", n=128), PQ3[:, :, 0:128], AF.Copy))
        P.op("s", lambda e, PQ3=PQ3, s4=s4: e.activation(
            k2f[s4].rearrange("p (h n) -> p h n", n=128), PQ3[:, :, 128:256], AF.Copy))
        tev = P.op("s", lambda e, PQ3=PQ3, s8=s8: e.activation(
            v2[s8].rearrange("p (h n) -> p h n", n=128), PQ3[:, :, 256:384], AF.Copy),
            inc="act")
        st.bank_free[bks[0]] = tev
        st.bank_free[bks[1]] = tev
        tevs[i] = tev
        P.wait("v", tev)
        P.wait("v", st.misc.get(("qrkr_free", s4)))
        q3 = q2f[s4].rearrange("p (h n) -> p h n", n=128)
        t3 = tmpq[s4].rearrange("p (h n) -> p h n", n=128)
        c3 = qcf[s4].rearrange("p (h n) -> p h n", n=128)
        P.op("v", lambda e, q3=q3, t3=t3: e.tensor_tensor(
            t3[:, :, 0:64], q3[:, :, 64:128],
            t.sinm_q[:, 0:64].unsqueeze(1).broadcast_to((128, 2, 64)), ALU.mult))
        P.op("v", lambda e, q3=q3, t3=t3: e.tensor_tensor(
            t3[:, :, 64:128], q3[:, :, 0:64],
            t.sinm_q[:, 64:128].unsqueeze(1).broadcast_to((128, 2, 64)), ALU.mult))
        P.op("v", lambda e, q3=q3, c3=c3: e.tensor_tensor(
            c3, q3, t.cos_q[:].unsqueeze(1).broadcast_to((128, 2, 128)), ALU.mult))
        tq = P.op("v", lambda e, s4=s4: e.tensor_add(qr[s4], qcf[s4], tmpq[s4]),
                  inc="dve")
        k3 = k2f[s4].rearrange("p (h n) -> p h n", n=128)
        kr3 = kr[s4].rearrange("p (h n) -> p h n", n=128)
        P.op("v", lambda e, k3=k3, kr3=kr3: e.tensor_tensor(
            kr3[:, :, 0:64], k3[:, :, 64:128],
            t.sinm_k[:, 0:64].unsqueeze(1).broadcast_to((128, 2, 64)), ALU.mult))
        P.op("v", lambda e, k3=k3, kr3=kr3: e.tensor_tensor(
            kr3[:, :, 64:128], k3[:, :, 0:64],
            t.sinm_k[:, 64:128].unsqueeze(1).broadcast_to((128, 2, 64)), ALU.mult))
        P.op("v", lambda e, k3=k3: e.tensor_tensor(
            k3, k3, t.cos_k[:].unsqueeze(1).broadcast_to((128, 2, 128)), ALU.mult))
        tk = P.op("v", lambda e, s4=s4: e.tensor_tensor(
            kr[s4], k2f[s4], kr[s4], ALU.add), inc="dve")
        st.misc[("fscr_free", s4)] = tk
        state[("rope", i)] = (tq, tk)

    def stageB1a(i):
        s4 = i % 4
        pair = (i + 1) % 2
        PQp = t.PBIG0 if pair == 0 else t.PBIG1
        pv = bf16v(PQp[:])
        dsts = (pv[:, 768:896], pv[:, 896:1024], pv[:, 1792:1920], pv[:, 1920:2048])
        tq, tk = state.pop(("rope", i))
        P.wait("t", tq)
        P.wait("t", tk)
        P.wait("t", tevs.get(i + 1))
        P.wait("t", st.misc.get(("qtb_prev", pair)))
        srcs = (qr[s4][:, 0:128], qr[s4][:, 128:256],
                kr[s4][:, 0:128], kr[s4][:, 128:256])
        ttp = None
        for j in range(4):
            ttp = P.op("t", lambda e, dst=dsts[j], src=srcs[j], j=j: e.matmul(
                dst, src, t.id_bf[:],
                is_transpose=True, start=(j == 0), stop=(j == 3)),
                inc="pe" if j == 3 else None)
        st.misc[("qrkr_free", s4)] = ttp
        P.wait("v", ttp)
        P.wait("v", st.misc.get(("qkT_free", s4)))
        P.op("v", lambda e, pv=pv, s4=s4: e.tensor_copy(
            qkT[s4][:, 0:256], pv[:, 768:1024]))
        tqk = P.op("v", lambda e, pv=pv, s4=s4: e.tensor_copy(
            qkT[s4][:, 256:512], pv[:, 1792:2048]), inc="dve")
        st.misc[("qtb_free", pair)] = tqk
        st.misc[("qtb_prev", pair)] = tqk
        state[("tqk", i)] = tqk

    def stageB1b(i):
        s4 = i % 4
        sb_ = 4 + i % 4
        tqk = state.pop(("tqk", i))
        P.wait("t", tqk)
        P.wait("t", st.misc.get(("mid_free", i % 4)))
        tsc = None
        for h01 in range(2):
            P.op("t", lambda e, s4=s4, h01=h01, sb_=sb_: e.matmul(
                t.BK[sb_][:, h01 * 128:(h01 + 1) * 128],
                qkT[s4][:, h01 * 128:(h01 + 1) * 128],
                qkT[s4][:, 256 + h01 * 128:256 + (h01 + 1) * 128],
                start=(h01 == 0), stop=False))
            tsc = P.op("t", lambda e, h01=h01, sb_=sb_: e.matmul(
                t.BK[sb_][:, h01 * 128:(h01 + 1) * 128], t.id_bf[:], t.mask_bf[:],
                start=False, stop=(h01 == 1)), inc="pe" if h01 == 1 else None)
        st.misc[("qkT_free", s4)] = tsc
        state[("tsc", i)] = tsc

    def stageB2a(i):
        s4 = i % 4
        par = i % 2
        sb_ = 4 + i % 4
        tsc = state.pop(("tsc", i))
        C = t.smallc[:, par * 256:(par + 1) * 256]
        negmax2 = C[:, 224:226]
        rs2 = C[:, 226:228]
        rec2 = C[:, 228:230]
        P.wait("v", tsc)
        P.wait("v", st.misc.get(("sm_free", par)))
        dnm = P.vs(lambda e, negmax2=negmax2, sb_=sb_: e.reduce_max(
            negmax2, t.BK[sb_].rearrange("p (h n) -> p h n", n=128)[:, 0:2, :],
            axis=AX.X, negate=True))
        P.wait("s", dnm)
        P.wait("s", st.misc.get(("e2f_free", s4)))
        tse = None
        for h01 in range(2):
            tse = P.op("s", lambda e, h01=h01, s4=s4, sb_=sb_, negmax2=negmax2, rs2=rs2:
                       e.activation(
                           e2f[s4][:, h01 * 128:(h01 + 1) * 128],
                           t.BK[sb_][:, h01 * 128:(h01 + 1) * 128], AF.Exp,
                           bias=negmax2[:, h01:h01 + 1],
                           accum_out=rs2[:, h01:h01 + 1]),
                       inc="act" if h01 == 1 else None)
        st.misc[("sc_free", par)] = tse
        P.wait("v", tse)
        P.vs(lambda e, rec2=rec2, rs2=rs2: e.reciprocal(rec2, rs2))
        P.wait("v", st.misc.get(("pbf_free", s4)))
        tpn = None
        for h01 in range(2):
            tpn = P.op("v", lambda e, s4=s4, h01=h01, rec2=rec2: e.tensor_scalar_mul(
                pbf[s4][:, h01 * 128:(h01 + 1) * 128],
                e2f[s4][:, h01 * 128:(h01 + 1) * 128],
                rec2[:, h01:h01 + 1]), inc="dve" if h01 == 1 else None)
        st.misc[("sm_free", par)] = tpn
        st.misc[("e2f_free", s4)] = tpn
        state[("tpn", i)] = tpn

    def stageB2b1(i):
        s4 = i % 4
        sb_ = 4 + i % 4
        tpn = state.pop(("tpn", i))
        P.wait("t", tpn)
        pv = bf16v(t.BK[sb_])
        tpt = None
        for j in range(2):
            tpt = P.op("t", lambda e, pv=pv, j=j, s4=s4: e.matmul(
                pv[:, 512 + j * 128:512 + (j + 1) * 128],
                pbf[s4][:, j * 128:(j + 1) * 128],
                t.id_bf[:], is_transpose=True, start=(j == 0), stop=(j == 1)),
                inc="pe" if j == 1 else None)
        st.misc[("pbf_free", s4)] = tpt
        P.wait("v", tpt)
        P.wait("v", st.misc.get(("pT_free", s4)))
        tptc = P.op("v", lambda e, pv=pv, s4=s4: e.tensor_copy(pT[s4], pv[:, 512:768]),
                    inc="dve")
        state[("tptc", i)] = tptc

    def stageB2b2(i):
        hp, tt = i // NTILE, i % NTILE
        s4 = i % 4
        s8 = i % 8
        sb_ = 4 + i % 4
        tptc = state.pop(("tptc", i))
        P.wait("t", tptc)
        tcx = None
        for h01 in range(2):
            tcx = P.op("t", lambda e, s8=s8, s4=s4, h01=h01, sb_=sb_: e.matmul(
                t.BK[sb_][:, 256 + h01 * 128:256 + (h01 + 1) * 128],
                v2[s8][:, h01 * 128:(h01 + 1) * 128],
                pT[s4][:, h01 * 128:(h01 + 1) * 128],
                start=(h01 == 0), stop=(h01 == 1)), inc="pe" if h01 == 1 else None)
        st.misc[("pT_free", s4)] = tcx
        st.misc[("v2_free", s8)] = tcx
        P.wait("s", tcx)
        tcc = None
        for h01 in range(2):
            slot = (hp * 2 + h01) * 4 + tt
            tcc = P.op("s", lambda e, slot=slot, h01=h01, sb_=sb_: e.activation(
                ctxT3[:, slot, :], t.BK[sb_][:, 256 + h01 * 128:256 + (h01 + 1) * 128],
                AF.Copy), inc="act" if h01 == 1 else None)
        st.misc[("mid_free", i % 4)] = tcc
        st.bank_free[sb_] = tcc
        return tcc

    tcc = None
    for k in range(NIT + 6):
        if k >= 6:
            tcc = stageB2b2(k - 6)
        if 5 <= k < NIT + 5:
            stageB2b1(k - 5)
        if 4 <= k < NIT + 4:
            stageB2a(k - 4)
        if 3 <= k < NIT + 3:
            stageB1b(k - 3)
        if 2 <= k < NIT + 2:
            stageB1a(k - 2)
        if k < NIT:
            stageA(k)
    st.misc["ctxT_done"] = tcc
    st.misc["l1_pe_end"] = P.now("pe")
    st.big_r.add(P.now("pe"))
    st.scr_r.add(P.now("dve"))
    st.scr_r.add(P.now("pool"))
    st.scr_r.add(P.now("act"))


def emit_gh_pre(P, t, st, m):
    txa = {}
    xas = [bigbf(t, 31744 + tt * 128, 128) for tt in range(NTILE)]
    for tt in range(NTILE):
        par = tt % 2
        bk = 4 + par
        C = t.smallc[:, par * 256:(par + 1) * 256]
        logit8 = C[:, 168:176]
        u8 = C[:, 176:184]
        is8 = C[:, 184:192]
        msk8 = C[:, 192:200]
        m1n = C[:, 200:201]
        m12 = C[:, 201:202]
        es1 = C[:, 202:203]
        zc = C[:, 203:204]
        rZ = C[:, 204:205]
        k8a = C[:, 208:216]
        w8 = C[:, 216:224]
        e8 = C[:, 232:240]
        P.wait("t", st.xnT_ready)
        P.wait("t", st.misc.get("setup_g"))
        P.wait("t", st.bank_free[bk])
        for kc in range(KC):
            P.op("t", lambda e, kc=kc, tt=tt, bk=bk: e.matmul(
                t.BK[bk][:, 0:136],
                t.xnT[:, kc * TOK + tt * 128:kc * TOK + tt * 128 + 128],
                t.gwAT_bf[:, kc * 136:(kc + 1) * 136],
                start=(kc == 0), stop=False))
        tla = P.op("t", lambda e, bk=bk: e.matmul(
            t.BK[bk][:, 0:8], t.ones_bf[:], t.rows_bf[:, 4096:4104],
            start=False, stop=True), inc="pe")
        P.wait("v", tla)
        P.wait("v", st.misc.get(("gate_free", par)))
        P.vs(lambda e, logit8=logit8, bk=bk: e.tensor_copy(logit8, t.BK[bk][:, 0:8]))
        P.vs(lambda e, m1n=m1n, logit8=logit8: e.reduce_max(
            m1n, logit8, axis=AX.X, negate=True))
        P.vs(lambda e, u8=u8, logit8=logit8, m1n=m1n: e.tensor_scalar_add(
            u8, logit8, m1n))
        P.vs(lambda e, is8=is8, u8=u8: e.tensor_single_scalar(
            is8, u8, 0.0, ALU.is_ge))
        P.vs(lambda e, msk8=msk8, is8=is8, u8=u8: e.scalar_tensor_tensor(
            msk8, is8, -1e30, u8, ALU.mult, ALU.add))
        dm = P.vs(lambda e, m12=m12, msk8=msk8: e.reduce_max(
            m12, msk8, axis=AX.X, negate=True))
        P.wait("s", dm)
        P.op("s", lambda e, e8=e8, u8=u8: e.activation(e8, u8, AF.Exp))
        sa = P.op("s", lambda e, es1=es1, m12=m12: e.activation(
            es1, m12, AF.Exp, scale=-1.0), inc="act")
        P.wait("v", sa)
        P.vs(lambda e, zc=zc, es1=es1: e.tensor_scalar_add(zc, es1, 1.0))
        P.vs(lambda e, rZ=rZ, zc=zc: e.reciprocal(rZ, zc))
        P.vs(lambda e, k8a=k8a, u8=u8, m12=m12: e.tensor_scalar_add(k8a, u8, m12))
        P.vs(lambda e, k8a=k8a: e.tensor_single_scalar(k8a, k8a, 0.0, ALU.is_ge))
        P.vs(lambda e, w8=w8, e8=e8, k8a=k8a: e.tensor_mul(w8, e8, k8a))
        P.vs(lambda e, w8=w8, rZ=rZ: e.tensor_scalar_mul(w8, w8, rZ))
        tx = None
        for ex in range(8):
            tx = P.op("v", lambda e, ex=ex, xa=xas[tt], w8=w8, bk=bk:
                      e.tensor_scalar_mul(
                          xa[:, ex * 16:(ex + 1) * 16],
                          t.BK[bk][:, 8 + ex * 16:8 + (ex + 1) * 16],
                          w8[:, ex:ex + 1]), inc="dve" if ex == 7 else None)
        txa[tt] = tx
        st.bank_free[bk] = tx
        st.misc[("gate_free", par)] = tx
    for tt in range(NTILE):
        b = 6 + tt % 2
        P.wait("t", txa[tt])
        P.wait("t", st.bank_free[b])
        txt = P.op("t", lambda e, xa=xas[tt], b=b: e.matmul(
            bf16v(t.BK[b])[:, 0:128], xa, t.id_bf[:],
            is_transpose=True, start=True, stop=True), inc="pe")
        P.wait("v", txt)
        txc = P.op("v", lambda e, tt=tt, b=b: e.tensor_copy(
            t.xascT[:, tt * 128:(tt + 1) * 128], bf16v(t.BK[b])[:, 0:128]), inc="dve")
        st.bank_free[b] = txc
        st.misc[("xascT_tok", tt)] = txc

    if "GPM" in DBG_BARRIER:
        bar(P)

def emit_gh(P, t, st, m, next_src=None, nm=None):
    war = st.big_r.toks()
    st.big_r.clear()
    scr_war = st.scr_r.toks()
    st.scr_r.clear()
    WTb = [bigbf(t, 0, 8192), bigbf(t, 8192, 8192)]
    wdnb = [bigbf(t, 16384, 8192), bigbf(t, 24576, 8192)]
    osb = [t.scr32[:, 0:512], t.scr32[:, 512:1024]]
    qkv0 = st.misc.get(("wqkv_pe", 0))
    qkv1 = st.misc.get(("wqkv_pe", 1))
    l1end = st.misc.get("l1_pe_end")
    region_tok = {("WT", 0): (qkv0,), ("WT", 1): (qkv0, qkv1),
                  ("wdn", 0): (qkv1,), ("wdn", 1): (l1end,)}
    ghw_tok = {}

    def issue_ghw(oc):
        pb = oc % 2
        P.waits("y", region_tok[("WT", pb)])
        P.wait("y", st.misc.get(("WT_pe", pb)))
        P.waits("y", st.misc.get(("conv", "WT"), ()))
        tok_WT = P.dma("y", "WT%d" % pb, lambda e, oc=oc, pb=pb: e.dma_start(
            WTb[pb].rearrange("p (k n) -> p k n", n=512),
            t.WTS.rearrange("(k p) n -> p k n", p=128)[:, :, oc * 512:(oc + 1) * 512]))
        P.waits("y", region_tok[("wdn", pb)])
        P.wait("y", st.misc.get(("wdn_pe", pb)))
        P.waits("y", st.misc.get(("conv", "wdn"), ()))
        tok_wdn = P.dma("y", "wdn%d" % pb, lambda e, oc=oc, pb=pb: e.dma_start(
            wdnb[pb].rearrange("p (k n) -> p k n", n=512),
            t.wdnS.rearrange("(k p) n -> p k n", p=128)[:, :, oc * 512:(oc + 1) * 512]))
        ghw_tok[oc] = (tok_WT, tok_wdn)

    issue_ghw(0)
    issue_ghw(1)
    for oc in range(4):
        pb = oc % 2
        tok_WT, tok_wdn = ghw_tok.pop(oc)
        for tt in range(NTILE):
            par = (oc * NTILE + tt) % 2
            bk = 0 if par == 0 else 2
            P.wait("t", tok_WT)
            P.wait("t", tok_wdn)
            P.wait("t", st.misc.get("ctxT_done"))
            P.wait("t", st.misc.get(("xascT_tok", tt)))
            P.wait("t", st.bank_free[bk])
            for h in range(16):
                P.op("t", lambda e, h=h, tt=tt, pb=pb, bk=bk: e.matmul(
                    t.BK[bk], t.ctxT[:, (h * 4 + tt) * 128:(h * 4 + tt) * 128 + 128],
                    wdnb[pb][:, h * 512:(h + 1) * 512],
                    start=(h == 0), stop=False))
            for kc in range(KC):
                P.op("t", lambda e, kc=kc, tt=tt, pb=pb, bk=bk: e.matmul(
                    t.BK[bk], t.xnT[:, kc * TOK + tt * 128:kc * TOK + tt * 128 + 128],
                    WTb[pb][:, kc * 512:(kc + 1) * 512],
                    start=False, stop=False))
            P.op("t", lambda e, tt=tt, oc=oc, bk=bk: e.matmul(
                t.BK[bk], t.xascT[:, tt * 128:(tt + 1) * 128],
                t.ballT_bf[:, oc * 512:(oc + 1) * 512],
                start=False, stop=False))
            td = P.op("t", lambda e, oc=oc, bk=bk: e.matmul(
                t.BK[bk], t.ones_bf[:], t.rows_bf[:, 2048 + oc * 512:2048 + (oc + 1) * 512],
                start=False, stop=True), inc="pe")
            st.misc[("WT_pe", pb)] = td
            st.misc[("wdn_pe", pb)] = td
            P.wait("v", td)
            P.waits("v", scr_war)
            P.wait("v", st.misc.get(("osb_free", par)))
            tvo = P.op("v", lambda e, tt=tt, oc=oc, par=par, bk=bk: e.tensor_add(
                osb[par], t.BK[bk],
                t.x1[:, tt * D + oc * 512:tt * D + (oc + 1) * 512]), inc="dve")
            st.bank_free[bk] = tvo
            st.x1_r[tt].add(tvo)
            j0 = m * MJ + tt * 8
            P.wait("y", tvo)
            tst = P.dma("y", "os%d" % par, lambda e, j0=j0, oc=oc, par=par: e.dma_start(
                t.out.transpose([1, 0, 2])[j0:j0 + 8][:, :, oc * 512:(oc + 1) * 512],
                osb[par]))
            st.misc[("osb_free", par)] = tst
            if oc == 3 and next_src is not None:
                j0n = nm * MJ + tt * 8
                P.waits("y", st.x1_r[tt].toks())
                P.wait("y", st.x1_w[tt])
                tok = P.dma("y", "x%d" % tt, lambda e, j0n=j0n, tt=tt: e.dma_start(
                    t.x1[:, tt * D:(tt + 1) * D],
                    next_src.transpose([1, 0, 2])[j0n:j0n + 8]))
                st.x1_w[tt] = tok
                st.x1_r[tt].clear()
        if oc + 2 < 4:
            issue_ghw(oc + 2)
    st.big_r.add(P.now("pe"))
    st.scr_r.add(st.misc.get(("osb_free", 0)))
    st.scr_r.add(st.misc.get(("osb_free", 1)))
    st.scr_r.add(P.now("dve"))


def build(mode, dbg=0):
    layer0 = mode in ("l0", "fused")
    layer1 = mode in ("l1", "fused")
    nc = bass.Bass("TRN2", target_bir_lowering=False)
    t = T()
    es = contextlib.ExitStack()
    with es:
        _alloc(nc, es, t, layer0, layer1)
        if dbg:
            t.dbg_xnT = nc.dram_tensor("dbg_xnT", [dbg, 128, KC * TOK], BF16,
                                       kind="ExternalOutput").ap()
            t.dbg_x1b = nc.dram_tensor("dbg_x1b", [dbg, 128, NTILE * D], F32,
                                       kind="ExternalOutput").ap()
            t.dbg_fuse = nc.dram_tensor("dbg_fuse", [dbg, 128, 32768], BF16,
                                        kind="ExternalOutput").ap()
        P = Rec(nc, es)
        st = St()
        emit_setup(P, t, st, layer0, layer1)

        def dump(dst, src_ap, toks):
            P.waits("y", toks)
            return P.dma("y", "dbg", lambda e: e.dma_start(dst, src_ap))

        P.wait("t", st.misc.get("setup_g"))
        P.wait("t", st.misc.get("setup_y"))
        xsrc = t.x if layer0 else t.x1i
        for m in range(NMEGA):
            if layer0:
                if m == 0 or not layer1:
                    emit_x_load(P, t, st, m, t.x)
                emit_ln(P, t, st, m, "big")
                if dbg and m < dbg:
                    st.xnT_r.add(dump(t.dbg_xnT[m], t.xnT[:], [st.xnT_ready]))
                emit_attn0(P, t, st, m)
                if dbg and m < dbg:
                    tk = dump(t.dbg_x1b[m], t.x1[:],
                              [st.x1_w[i] for i in range(NTILE)])
                    for i in range(NTILE):
                        st.x1_r[i].add(tk)
                emit_ffn0(P, t, st, m)
                if m == 0 and layer1:
                    cv = st.misc["convert_fn"]
                    st.misc[("conv", "wqkv")] = cv(t.wqkv, t.wqkvS, 3 * D)
                    st.misc[("conv", "wdn")] = cv(t.wdense, t.wdnS, D)
                    st.misc[("conv", "WT")] = cv(t.WT, t.WTS, D)
                    st.misc["stage_free"] = st.misc["conv_out_tok"]
                if dbg and m < dbg:
                    st.big_r.add(dump(t.dbg_fuse[m], t.big[:],
                                      [st.fuse_ready[i] for i in range(NIC)]))
                if not layer1:
                    emit_x1_store(P, t, st, m)
            if layer1:
                if not layer0 and m == 0:
                    emit_x_load(P, t, st, m, t.x1i)
                emit_ln(P, t, st, m, "wC")
                if "EF" in DBG_BARRIER:
                    bar(P)
                emit_gh_pre(P, t, st, m)
                emit_l1attn(P, t, st, m)
                if "FG" in DBG_BARRIER:
                    bar(P)
                nxt = xsrc if m + 1 < NMEGA else None
                emit_gh(P, t, st, m, next_src=nxt, nm=m + 1)
        engsems = {"pe", "dve", "act", "pool"}
        for name in list(P.sems):
            if name not in engsems:
                P.q["y"].append(lambda e, P=P, n=name: e.wait_ge(P.sems[n], P.cnt[n]))
        with nc.Block() as block:
            @block.tensor
            def _(e):
                for fn in P.q["t"]:
                    fn(e)

            @block.vector
            def _(e):
                for fn in P.q["v"]:
                    fn(e)

            @block.scalar
            def _(e):
                for fn in P.q["s"]:
                    fn(e)

            @block.gpsimd
            def _(e):
                for fn in P.q["g"]:
                    fn(e)

            @block.sync
            def _(e):
                for fn in P.q["y"]:
                    fn(e)
    return nc


def _np32(a):
    return np.ascontiguousarray(np.asarray(a, dtype=np.float32))


def _rope_tables():
    dim = 128
    inv_freq = (1.0 / (np.float32(10000.0) **
                       (np.arange(0, dim, 2, dtype=np.float32) / np.float32(dim)))).astype(np.float32)
    tpos = np.arange(NI, dtype=np.float32)
    freqs = np.einsum("s,d->sd", tpos, inv_freq).astype(np.float32)
    emb = np.concatenate([freqs, freqs], axis=-1)
    cos = np.cos(emb).astype(np.float32)
    sin = np.sin(emb).astype(np.float32)
    cos_t = np.tile(cos, (8, 1)).astype(np.float32)
    sin_t = np.tile(sin, (8, 1)).astype(np.float32)
    sinm = sin_t.copy()
    sinm[:, 0:64] *= np.float32(-1.0)
    scale = np.float32(1.0 / math.sqrt(128.0))
    return (np.ascontiguousarray(cos_t * scale), np.ascontiguousarray(sinm * scale),
            np.ascontiguousarray(cos_t), np.ascontiguousarray(sinm))


def _mask_const():
    m = np.full((128, 128), -10000.0, dtype=np.float32)
    for jj in range(8):
        for s_ in range(16):
            m[jj * 16 + s_, jj * 16:jj * 16 + s_ + 1] = 0.0
    return m


def _common_inputs(inputs):
    d = {}
    d["Wq"] = _np32(inputs["L0_attn_Wq"])
    d["Wkv"] = _np32(inputs["L0_attn_Wkv"])
    d["wg8"] = np.ascontiguousarray(np.asarray(inputs["L0_ffn_w_gate"], dtype=np.int8))
    d["wu8"] = np.ascontiguousarray(np.asarray(inputs["L0_ffn_w_up"], dtype=np.int8))
    d["wd8"] = np.ascontiguousarray(np.asarray(inputs["L0_ffn_w_down"], dtype=np.int8))
    d["sg_pc"] = np.ascontiguousarray(_np32(inputs["L0_ffn_s_gate"]).reshape(64, 128).T)
    d["su_pc"] = np.ascontiguousarray(_np32(inputs["L0_ffn_s_up"]).reshape(64, 128).T)
    d["sd_r"] = _np32(inputs["L0_ffn_s_down"]).reshape(1, D)
    d["bkv_r"] = _np32(inputs["L0_attn_bkv"]).reshape(1, 512)
    d["wqkv"] = _np32(inputs["L1_attn_w_qkv"])
    d["wdense"] = _np32(inputs["L1_attn_w_dense"])
    d["WT"] = np.ascontiguousarray(_np32(inputs["L1_ffn_W"]).T)
    gw = _np32(inputs["L1_ffn_gate_w"])
    lA = _np32(inputs["L1_ffn_lora_A"]).reshape(128, D)
    d["gwAT"] = np.ascontiguousarray(np.concatenate([gw.T, lA.T], axis=1))
    lB = _np32(inputs["L1_ffn_lora_B"])
    d["ballT"] = np.ascontiguousarray(np.transpose(lB, (0, 2, 1)).reshape(128, D))
    cq, sq, ck, sk = _rope_tables()
    d["cosq"], d["sinmq"], d["cosk"], d["sinmk"] = cq, sq, ck, sk
    d["maskc"] = _mask_const()
    d["rows3"] = np.ascontiguousarray(np.concatenate([
        _np32(inputs["L0_attn_bq"]).reshape(-1),
        _np32(inputs["L1_ffn_b"]).reshape(-1),
        _np32(inputs["L1_ffn_gate_b"]).reshape(-1)]).reshape(1, 4104))
    d["identc"] = np.eye(128, dtype=np.float32)
    d["onesc"] = np.ones((1, 128), dtype=np.float32)
    return d


L0_KEYS = {"x", "Wq", "Wkv", "wg8", "wu8", "wd8", "sg_pc", "su_pc", "sd_r",
           "bkv_r", "rows3", "identc", "onesc"}
L1_KEYS = {"x1i", "wqkv", "wdense", "WT", "gwAT", "ballT", "cosq", "sinmq",
           "cosk", "sinmk", "maskc", "rows3", "identc", "onesc"}

MODE = "fused"


def kernel(**inputs):
    common = _common_inputs(inputs)
    x = _np32(inputs["L0_attn_x"])
    xs = [np.ascontiguousarray(x[:, c * JC:(c + 1) * JC, :]) for c in range(NCORES)]

    if MODE == "2launch":
        nc0 = build("l0")
        in0 = [{k: v for k, v in dict(common, x=xs[c]).items() if k in L0_KEYS}
               for c in range(NCORES)]
        r0 = run_bass_kernel_spmd(nc0, in0, core_ids=list(range(NCORES)))
        x1s = [r0.results[c]["x1o"] for c in range(NCORES)]
        nc1 = build("l1")
        in1 = [dict({k: v for k, v in common.items() if k in L1_KEYS}, x1i=x1s[c])
               for c in range(NCORES)]
        r1 = run_bass_kernel_spmd(nc1, in1, core_ids=list(range(NCORES)))
        outs = [r1.results[c]["out"] for c in range(NCORES)]
    else:
        ncf = build("fused")
        ins = [dict(common, x=xs[c]) for c in range(NCORES)]
        rf = run_bass_kernel_spmd(ncf, ins, core_ids=list(range(NCORES)))
        outs = [rf.results[c]["out"] for c in range(NCORES)]

    return np.concatenate(outs, axis=1)
